# version 12
# baseline (speedup 1.0000x reference)
import numpy as np
from contextlib import ExitStack

import concourse.bass as bass
import concourse.mybir as mybir
from concourse.bass_utils import run_bass_kernel_spmd

F32 = mybir.dt.float32
BF16 = mybir.dt.bfloat16
I8 = mybir.dt.int8
AF = mybir.ActivationFunctionType
ALU = mybir.AluOpType

D = 2048
S = 4096
NKV = 8
INTER = 8192
NCH = D // 128
EPS = 1e-5
ESHIFT = -30.0

ENGINES = ("pe", "dve", "act", "pool", "sync")


class Prog:

    def __init__(self):
        self.ops = {e: [] for e in ENGINES}
        self.cnt = {"pe": 0, "dve": 0, "act": 0, "pool": 0, "qs": 0, "qp": 0}
        self.last_w = {}
        self.readers = {}
        self.waited = {e: {} for e in ENGINES}

    def op(self, engine, emit, reads=(), writes=(), dma=None):
        rd, wr = [], list(writes)
        for k in reads:
            if isinstance(k, tuple) and k and k[0] == "pb":
                wr.append(k)
            else:
                rd.append(k)
        need = {}

        def _need(sem, val):
            if val > need.get(sem, 0):
                need[sem] = val

        for k in rd:
            w = self.last_w.get(k)
            if w:
                _need(*w)
        for k in wr:
            w = self.last_w.get(k)
            if w:
                _need(*w)
            for (rsem, rcnt) in self.readers.get(k, ()):
                if dma is None and rsem == engine:
                    continue
                _need(rsem, rcnt)
        waits = []
        for semn, val in need.items():
            if self.waited[engine].get(semn, 0) < val:
                waits.append((semn, val))
                self.waited[engine][semn] = val
        if dma:
            self.cnt[dma] += 16
            comp = (dma, self.cnt[dma])
            inc = (dma, 16)
        else:
            self.cnt[engine] += 1
            comp = (engine, self.cnt[engine])
            inc = (engine, 1)
        for k in wr:
            self.last_w[k] = comp
            self.readers[k] = []
        for k in rd:
            self.readers.setdefault(k, []).append(comp)
        self.ops[engine].append((waits, emit, inc))

    def fence(self, skip=()):
        for e in ENGINES:
            if e in skip:
                continue
            waits = []
            for semn, val in self.cnt.items():
                if val > 0 and self.waited[e].get(semn, 0) < val:
                    waits.append((semn, val))
                    self.waited[e][semn] = val
            if waits:
                self.ops[e].append((waits, None, None))

    def tail_wait(self, engine, sems):
        waits = []
        for semn in sems:
            val = self.cnt[semn]
            if val > 0 and self.waited[engine].get(semn, 0) < val:
                waits.append((semn, val))
                self.waited[engine][semn] = val
        if waits:
            self.ops[engine].append((waits, None, None))


DEBUG = False
LAST_RESULTS = None
kernel_overlap_p2 = True


def build_program():
    nc = bass.Bass("TRN2", target_bir_lowering=False)
    P = Prog()

    xq_d = nc.dram_tensor("xq", [128, NCH * 1024], F32, kind="ExternalInput").ap()
    xkv_d = nc.dram_tensor("xkv", [8, 128, NCH * 512], F32, kind="ExternalInput").ap()
    wq_d = nc.dram_tensor("wq", [32, 128, 1024], F32, kind="ExternalInput").ap()
    wk_d = nc.dram_tensor("wk", [NKV, 128, 1024], F32, kind="ExternalInput").ap()
    wv_d = nc.dram_tensor("wv", [NKV, 128, 1024], F32, kind="ExternalInput").ap()
    wo_d = nc.dram_tensor("wo", [16, 128, 512], F32, kind="ExternalInput").ap()
    wg_d = nc.dram_tensor("wg", [64, 2, 128, 1024], I8, kind="ExternalInput").ap()
    wu_d = nc.dram_tensor("wu", [64, 2, 128, 1024], I8, kind="ExternalInput").ap()
    wd_d = nc.dram_tensor("wd", [8, 2, 128, 32 * 256], I8, kind="ExternalInput").ap()
    sg_d = nc.dram_tensor("sg", [INTER], F32, kind="ExternalInput").ap()
    su_d = nc.dram_tensor("su", [INTER], F32, kind="ExternalInput").ap()
    sd_d = nc.dram_tensor("sd", [D], F32, kind="ExternalInput").ap()
    mba_d = nc.dram_tensor("mba", [128, 12], F32, kind="ExternalInput").ap()
    mbb_d = nc.dram_tensor("mbb", [128, 24], F32, kind="ExternalInput").ap()
    out_d = nc.dram_tensor("out", [128, NCH * 1024], F32, kind="ExternalOutput").ap()
    if DEBUG:
        dbg_kc = nc.dram_tensor("dbg_kc", [128, 4 * S], BF16,
                                kind="ExternalOutput").ap()
        dbg_vc = nc.dram_tensor("dbg_vc", [128, 32 * 520], BF16,
                                kind="ExternalOutput").ap()
        dbg_qs = nc.dram_tensor("dbg_qs", [128, 4 * 1024], BF16,
                                kind="ExternalOutput").ap()
        dbg_zsb = nc.dram_tensor("dbg_zsb", [2, 128, 2048], BF16,
                                 kind="ExternalOutput").ap()
        dbg_ao = nc.dram_tensor("dbg_ao", [2, 128, NCH * 512], BF16,
                                kind="ExternalOutput").ap()
        dbg_xattn = nc.dram_tensor("dbg_xattn", [128, NCH * 1024], F32,
                                   kind="ExternalOutput").ap()
        dbg_wq = nc.dram_tensor("dbg_wq", [128, NCH * 512], BF16,
                                kind="ExternalOutput").ap()
        dbg_hln = nc.dram_tensor("dbg_hln", [2, 128, NCH * 512], BF16,
                                 kind="ExternalOutput").ap()
        dbg_dn = nc.dram_tensor("dbg_dn", [2, 128, NCH * 512], BF16,
                                kind="ExternalOutput").ap()

    with ExitStack() as top:
        SB = lambda *a, **k: top.enter_context(nc.sbuf_tensor(*a, **k))
        pbw = [top.enter_context(nc.psum_tensor(f"pbw{i}", [128, 1024], F32))
               for i in range(2)]
        pbs = [top.enter_context(nc.psum_tensor(f"pb{i}", [128, 512], F32))
               for i in range(4, 8)]
        pb = {}
        for i in range(4):
            pb[i] = pbw[i // 2].ap()[:, (i % 2) * 1024 // 2:(i % 2) * 512 + 512]
        for i in range(4):
            pb[4 + i] = pbs[i].ap()
        PB = lambda i: ("pb", i)
        sem = {name: top.enter_context(nc.semaphore(f"sem_{name}"))
               for name in ("pe", "dve", "act", "pool", "qs", "qp")}

        tri = SB("tri", [128, 896], BF16)
        ones_c_bf = SB("ones_c_bf", [128, 1], BF16)
        ones_r_f = SB("ones_r_f", [1, 128], F32)
        ones65r = SB("ones65r", [65, 64], F32)
        sg_sb = SB("sg_sb", [128, 64], F32)
        su_sb = SB("su_sb", [128, 64], F32)
        sd_sb = SB("sd_sb", [128, 16], F32)
        mba_sb = SB("mba_sb", [128, 12], F32)
        mbb_sb = SB("mbb_sb", [128, 24], F32)
        m2shared = SB("m2shared", [1, 512], F32)
        ROWS = []
        for ri in range(2):
            ROWS.append(dict(
                mrow_f=SB(f"mrow_f{ri}", [1, 512], F32),
                qrow_f=SB(f"qrow_f{ri}", [1, 512], F32),
                m2row=None,
                rrow_f=SB(f"rrow_f{ri}", [1, 512], F32),
                mrow_bf=SB(f"mrow_bf{ri}", [1, 512], BF16),
                rcol=SB(f"rcol{ri}", [128, 4], F32),
                rb_bc=SB(f"rb_bc{ri}", [128, 512], F32),
                mb_bc=SB(f"mb_bc{ri}", [128, 512], F32),
            ))
        cst = SB("cst", [128, 8], F32)
        CST_ESHIFT, CST_SC8, CST_INVD, CST_EPS, CST_NEG1 = 0, 1, 2, 3, 4

        attn_scope = ExitStack()
        SR = lambda *a, **k: attn_scope.enter_context(
            nc.sbuf_tensor(*a, side="right", **k))
        kc_sb = SR("kc_sb", [128, 4 * S], BF16)
        vc_sb = SR("vc_sb", [128, 32 * 520], BF16)
        qs_sb = SR("qs_sb", [128, 4 * 1024], BF16)

        def E(fn):
            return fn

        def ln_stats_rows(xsrc, sqbufs, ri=0, bx=6, bq=7):
            n = len(xsrc)
            for c in range(n):
                ap_c, k_c = xsrc[c]
                sq_ap, sq_key = sqbufs[c % 2]
                P.op("pe", lambda eng, ap=ap_c, c=c, n=n: eng.matmul(
                    out=pb[bx][0:1, 0:512], lhsT=ones_c_bf[:], rhs=ap,
                    start=(c == 0), stop=(c == n - 1)),
                    reads=[k_c, "ones"], writes=[PB(bx)])
                P.op("dve", lambda eng, ap=ap_c, sq=sq_ap: eng.tensor_tensor(
                    out=sq, in0=ap, in1=ap, op=ALU.mult),
                    reads=[k_c], writes=[sq_key])
                P.op("pe", lambda eng, sq=sq_ap, c=c, n=n: eng.matmul(
                    out=pb[bq][0:1, 0:512], lhsT=ones_c_bf[:], rhs=sq,
                    start=(c == 0), stop=(c == n - 1)),
                    reads=[sq_key, "ones"], writes=[PB(bq)])

        def ln_rows_finish(ri=0, bx=6, bq=7, want_bf=False, want_col=False,
                           want_mb=False, mb_bf=None, rb_bf=None):
            R = ROWS[ri]
            mrow_f, qrow_f, m2row = R["mrow_f"], R["qrow_f"], m2shared
            rrow_f, mrow_bf = R["rrow_f"], R["mrow_bf"]
            rcol, rb_bc, mb_bc = R["rcol"], R["rb_bc"], R["mb_bc"]
            K = lambda nm: (nm, ri) if nm != "m2row" else "m2row"
            P.op("act", lambda eng: eng.activation(
                out=mrow_f[:], in_=pb[bx][0:1, 0:512], func=AF.Copy,
                scale=cst[0:1, 2:3]), reads=[PB(bx), "cst"],
                writes=[K("mrow_f")])
            P.op("act", lambda eng: eng.activation(
                out=qrow_f[:], in_=pb[bq][0:1, 0:512], func=AF.Copy,
                scale=cst[0:1, 2:3]), reads=[PB(bq), "cst"],
                writes=[K("qrow_f")])
            P.op("dve", lambda eng: eng.tensor_tensor(
                out=m2row[:], in0=mrow_f[:], in1=mrow_f[:], op=ALU.mult),
                reads=[K("mrow_f")], writes=[K("m2row")])
            P.op("dve", lambda eng: eng.tensor_tensor(
                out=qrow_f[:], in0=qrow_f[:], in1=m2row[:], op=ALU.subtract),
                reads=[K("qrow_f"), K("m2row")], writes=[K("qrow_f")])
            P.op("act", lambda eng: eng.activation(
                out=m2row[:], in_=qrow_f[:], func=AF.Sqrt,
                bias=cst[0:1, 3:4]),
                reads=[K("qrow_f"), "cst"], writes=[K("m2row")])
            P.op("dve", lambda eng: eng.reciprocal(out=rrow_f[:], in_=m2row[:]),
                reads=[K("m2row")], writes=[K("rrow_f")])
            if want_bf:
                P.op("dve", lambda eng: eng.tensor_copy(
                    out=mrow_bf[:], in_=mrow_f[:]),
                    reads=[K("mrow_f")], writes=[K("mrow_bf")])
            P.op("pe", lambda eng: eng.matmul(
                out=pb[bx][:, 0:512], lhsT=ones_r_f[:], rhs=rrow_f[:],
                start=True, stop=True), reads=[K("rrow_f"), "ones"],
                writes=[PB(bx)])
            P.op("act", lambda eng: eng.activation(
                out=rb_bc[:], in_=pb[bx][:, 0:512], func=AF.Copy),
                reads=[PB(bx)], writes=[K("rb_bc")])
            if rb_bf is not None:
                ap, key = rb_bf
                P.op("dve", lambda eng, ap=ap: eng.tensor_copy(
                    out=ap, in_=rb_bc[:]), reads=[K("rb_bc")], writes=[key])
            if want_mb or mb_bf is not None:
                P.op("pe", lambda eng: eng.matmul(
                    out=pb[bx][:, 0:512], lhsT=ones_r_f[:], rhs=mrow_f[:],
                    start=True, stop=True), reads=[K("mrow_f"), "ones"],
                    writes=[PB(bx)])
                P.op("act", lambda eng: eng.activation(
                    out=mb_bc[:], in_=pb[bx][:, 0:512], func=AF.Copy),
                    reads=[PB(bx)], writes=[K("mb_bc")])
                if mb_bf is not None:
                    ap, key = mb_bf
                    P.op("dve", lambda eng, ap=ap: eng.tensor_copy(
                        out=ap, in_=mb_bc[:]), reads=[K("mb_bc")], writes=[key])
            if want_col:
                for ts in range(4):
                    P.op("pe", lambda eng, ts=ts: eng.transpose(
                        out=pb[bx][0:128, ts:ts + 1],
                        in_=rrow_f[0:1, ts * 128:(ts + 1) * 128],
                        identity=ones_r_f[0:1, 0:1]),
                        reads=[K("rrow_f"), "ones"], writes=[PB(bx)])
                P.op("dve", lambda eng: eng.tensor_copy(
                    out=rcol[:], in_=pb[bx][0:128, 0:4]),
                    reads=[PB(bx)], writes=[K("rcol")])

        P.op("sync", lambda eng: eng.dma_start(out=mba_sb[:], in_=mba_d),
             writes=["mba"], dma="qs")
        P.op("sync", lambda eng: eng.dma_start(out=mbb_sb[:], in_=mbb_d),
             writes=["mbb"], dma="qs")
        for (dst, src, n, nm) in ((sg_sb, sg_d, 64, "sg"), (su_sb, su_d, 64, "su"),
                                  (sd_sb, sd_d, 16, "sd")):
            for c in range(n):
                src_ap = bass.AP(src.tensor, c * 128, [[1, 128], [1, 1]])
                P.op("sync", lambda eng, d=dst, c=c, s=src_ap: eng.dma_start(
                    out=d[:, c:c + 1], in_=s), writes=[(nm, c)], dma="qs")
        with ExitStack() as msk_scope:
            ncq = msk_scope.enter_context(nc.sbuf_tensor("ncq", [1, 512], BF16))
            nck = msk_scope.enter_context(nc.sbuf_tensor("nck", [1, 512], BF16))
            ncv = msk_scope.enter_context(nc.sbuf_tensor("ncv", [1, 512], BF16))
            wt = [msk_scope.enter_context(
                nc.sbuf_tensor(f"wt{i}", [128, 512], F32)) for i in range(4)]
            iden_f = msk_scope.enter_context(
                nc.sbuf_tensor("iden_f", [128, 128], F32))
            tmp_scope = ExitStack()
            itmp = tmp_scope.enter_context(
                nc.sbuf_tensor("itmp", [128, 896], F32))
            P.op("pool", lambda eng: eng.iota(
                itmp[:, 0:896], [[1, 896]], base=0, channel_multiplier=-1,
                allow_small_or_imprecise_dtypes=True), writes=["itmp"])
            P.op("pool", lambda eng: eng.tensor_scalar(
                out=tri[:], in0=itmp[:, 0:896], scalar1=384.0, scalar2=None,
                op0=ALU.is_ge), reads=["itmp"], writes=["tri"])
            P.op("pool", lambda eng: eng.iota(
                itmp[:, 0:128], [[1, 128]], base=0, channel_multiplier=-1,
                allow_small_or_imprecise_dtypes=True), writes=["itmp"])
            P.op("pool", lambda eng: eng.tensor_scalar(
                out=iden_f[:], in0=itmp[:, 0:128], scalar1=0.0, scalar2=None,
                op0=ALU.is_equal), reads=["itmp"], writes=["iden_f"])
            P.op("dve", lambda eng: eng.memset(ones_c_bf[:], 1.0),
                 writes=["ones"])
            P.op("dve", lambda eng: eng.memset(ones_r_f[:], 1.0),
                 writes=["ones"])
            P.op("dve", lambda eng: eng.memset(ones65r[64:65, :], 1.0),
                 writes=["ones"])
            for col, val in ((0, ESHIFT), (1, 0.125), (2, 1.0 / D),
                             (3, EPS), (4, -1.0)):
                P.op("dve", lambda eng, col=col, val=val: eng.memset(
                    cst[:, col:col + 1], val), writes=["cst"])
            P.op("dve", lambda eng: eng.memset(vc_sb[:], 1.0),
                writes=["vc_ones"])

            wq_sb = msk_scope.enter_context(
                nc.sbuf_tensor("wq_sb", [128, NCH * 512], BF16, side="right"))
            wk_sb = msk_scope.enter_context(
                nc.sbuf_tensor("wk_sb", [128, NCH * 512], BF16, side="right"))
            wv_sb = msk_scope.enter_context(
                nc.sbuf_tensor("wv_sb", [128, NCH * 512], BF16, side="right"))
            for n in range(NKV):
                for (wsb, wd_, nm) in ((wk_sb, wk_d, "wkk"), (wv_sb, wv_d, "wvv")):
                    dstv = wsb.ap().rearrange(
                        "p (c m) -> p c m", c=NCH)[:, :, n * 64:(n + 1) * 64]
                    P.op("pool", lambda eng, d=dstv, wd_=wd_, n=n: eng.dma_start(
                        out=d, in_=wd_[n].rearrange("p (c k) -> p c k", c=NCH)),
                        writes=[(nm, n)], dma="qp")
            def wq_head_chain(n):
                for half in range(2):
                    for g in range(4):
                        P.op("sync", lambda eng, n=n, g=g, half=half:
                             eng.dma_start(
                                 out=wt[g][:],
                                 in_=wq_d[4 * n + g][:, half * 512:
                                                     (half + 1) * 512]),
                             writes=[("wt", g)], dma="qs")

                    def smm(eng):
                        r = None
                        for g in range(4):
                            r = eng.matmul(
                                out=pb[6][:, 0:512], lhsT=iden_f[:],
                                rhs=wt[g][:],
                                start=(g == 0), stop=(g == 3))
                        return r
                    P.op("pe", smm,
                         reads=[("wt", g) for g in range(4)] + ["iden_f"],
                         writes=[PB(6)])
                    ws, wsk = vstage[half], ("vst", half)
                    P.op("dve", lambda eng, ws=ws: eng.tensor_copy(
                        out=ws[:], in_=pb[6][:, 0:512]),
                        reads=[PB(6)], writes=[wsk])
                    dst = wq_sb.ap().rearrange(
                        "p (c m) -> p c m",
                        c=NCH)[:, half * 8:(half + 1) * 8, n * 64:(n + 1) * 64]
                    P.op("sync", lambda eng, d=dst, ws=ws: eng.dma_start(
                        out=d, in_=ws.ap().rearrange("p (c k) -> p c k", c=8)),
                        reads=[wsk], writes=[("wq", n)], dma="qs")
            def colsum(wsb, dstn, nm, wkey):
                for c in range(NCH):
                    P.op("pe", lambda eng, wsb=wsb, c=c: eng.matmul(
                        out=pb[7][0:1, 0:512], lhsT=ones_c_bf[:],
                        rhs=wsb[:, c * 512:(c + 1) * 512],
                        start=(c == 0), stop=(c == NCH - 1)),
                        reads=[(wkey, nn) for nn in range(NKV)] + ["ones"],
                        writes=[PB(7)])
                P.op("act", lambda eng, dstn=dstn: eng.activation(
                    out=dstn[:], in_=pb[7][0:1, 0:512], func=AF.Copy,
                    scale=cst[0:1, 4:5]), reads=[PB(7), "cst"], writes=[nm])

            colsum(wk_sb, nck, "nck", "wkk")
            colsum(wv_sb, ncv, "ncv", "wvv")

            P.fence(skip=("pe",))
            tmp_scope.close()
            with ExitStack() as p1:
                S1 = lambda *a, **k: p1.enter_context(nc.sbuf_tensor(*a, **k))
                xb = [S1(f"xb{i}", [128, NCH * 512], BF16) for i in range(2)]
                sqb = [S1(f"sqb{i}", [128, 512], BF16) for i in range(2)]
                vstage = [S1(f"vstage{i}", [128, 512], BF16) for i in range(2)]

                def ln_fold_tile(src_ap, tile_idx, kind):
                    bi = tile_idx % 2
                    ri = tile_idx % 2
                    bx, bq = (6, 7) if ri == 0 else (4, 5)
                    R = ROWS[ri]
                    mrow_bf, rb_bc, rcol = R["mrow_bf"], R["rb_bc"], R["rcol"]
                    kmb, krb, krc = ("mrow_bf", ri), ("rb_bc", ri), ("rcol", ri)
                    xbt, xbk = xb[bi], ("xb", bi)
                    P.op("pool", lambda eng: eng.dma_start(
                        out=xbt[:], in_=src_ap), writes=[xbk], dma="qp")
                    ln_stats_rows([(xbt[:, c * 512:(c + 1) * 512], xbk)
                                   for c in range(NCH)],
                                  [(sqb[ri][:], ("sq", ri)),
                                   (vstage[ri][:], ("vst", ri))],
                                  ri=ri, bx=bx, bq=bq)
                    ln_rows_finish(ri=ri, bx=bx, bq=bq, want_bf=True,
                                   want_col=(kind == "kv"))
                    if kind == "kv":
                        g = tile_idx
                        for hp in range(4):
                            bnk = hp % 2

                            def kmm(eng, hp=hp, bnk=bnk, xbt=xbt):
                                r = None
                                for c in range(NCH):
                                    r = eng.matmul(
                                        out=pb[bnk][:, 0:512],
                                        lhsT=wk_sb[:, c * 512 + hp * 128:
                                                   c * 512 + (hp + 1) * 128],
                                        rhs=xbt[:, c * 512:(c + 1) * 512],
                                        start=(c == 0), stop=False)
                                return r
                            P.op("pe", kmm,
                                 reads=[xbk] +
                                       [("wkk", nn) for nn in range(NKV)],
                                 writes=[PB(bnk)])
                            P.op("pe", lambda eng, hp=hp, bnk=bnk: eng.matmul(
                                out=pb[bnk][:, 0:512],
                                lhsT=nck[0:1, hp * 128:(hp + 1) * 128],
                                rhs=mrow_bf[:], start=False, stop=True,
                                skip_group_check=True),
                                reads=[kmb, "nck"], writes=[PB(bnk)])
                            P.op("dve", lambda eng, hp=hp, bnk=bnk, g=g:
                                 eng.tensor_tensor(
                                     out=kc_sb[:, hp * S + g * 512:
                                               hp * S + (g + 1) * 512],
                                     in0=pb[bnk][:, 0:512], in1=rb_bc[:],
                                     op=ALU.mult),
                                 reads=[PB(bnk), krb],
                                 writes=[("Kc", g, hp)])
                        for ts in range(4):
                            bnk = 2 + ts % 2

                            def vmm(eng, ts=ts, bnk=bnk, xbt=xbt):
                                r = None
                                for c in range(NCH):
                                    r = eng.matmul(
                                        out=pb[bnk][:, 0:512],
                                        lhsT=xbt[:, c * 512 + ts * 128:
                                                 c * 512 + (ts + 1) * 128],
                                        rhs=wv_sb[:, c * 512:(c + 1) * 512],
                                        start=(c == 0), stop=False)
                                return r
                            P.op("pe", vmm,
                                 reads=[xbk] +
                                       [("wvv", nn) for nn in range(NKV)],
                                 writes=[PB(bnk)])
                            P.op("pe", lambda eng, ts=ts, bnk=bnk: eng.matmul(
                                out=pb[bnk][:, 0:512],
                                lhsT=mrow_bf[0:1, ts * 128:(ts + 1) * 128],
                                rhs=ncv[:], start=False, stop=True,
                                skip_group_check=True),
                                reads=[kmb, "ncv"], writes=[PB(bnk)])
                            slot = g * 4 + ts
                            vs, vsk = vstage[ts % 2], ("vst", ts % 2)
                            P.op("dve", lambda eng, vs=vs, ts=ts, bnk=bnk:
                                 eng.tensor_scalar(
                                     out=vs[:], in0=pb[bnk][:, 0:512],
                                     scalar1=rcol[:, ts:ts + 1], scalar2=None,
                                     op0=ALU.mult),
                                 reads=[PB(bnk), krc], writes=[vsk])
                            vdst = vc_sb.ap()[:, slot * 520:(slot + 1) * 520] \
                                .rearrange("p (n h) -> p n h", n=8)[:, :, 0:64]
                            P.op("sync", lambda eng, vdst=vdst, vs=vs:
                                 eng.dma_start(
                                     out=vdst,
                                     in_=vs.ap().rearrange("p (n h) -> p n h",
                                                           n=8)),
                                 reads=[vsk, "vc_ones"],
                                 writes=[("Vc", slot)], dma="qs")
                    else:
                        blk = tile_idx - 8
                        for hp in range(4):
                            bnk = hp % 2

                            def qmm(eng, hp=hp, bnk=bnk, xbt=xbt):
                                r = None
                                for c in range(NCH):
                                    r = eng.matmul(
                                        out=pb[bnk][:, 0:512],
                                        lhsT=wq_sb[:, c * 512 + hp * 128:
                                                   c * 512 + (hp + 1) * 128],
                                        rhs=xbt[:, c * 512:(c + 1) * 512],
                                        start=(c == 0), stop=False)
                                return r
                            P.op("pe", qmm,
                                 reads=[xbk] +
                                       [("wq", nn) for nn in range(NKV)],
                                 writes=[PB(bnk)])
                            P.op("pe", lambda eng, hp=hp, bnk=bnk: eng.matmul(
                                out=pb[bnk][:, 0:512],
                                lhsT=ncq[0:1, hp * 128:(hp + 1) * 128],
                                rhs=mrow_bf[:], start=False, stop=True,
                                skip_group_check=True),
                                reads=[kmb, "ncq"], writes=[PB(bnk)])
                            P.op("dve", lambda eng, hp=hp, bnk=bnk, blk=blk:
                                 eng.tensor_tensor(
                                     out=qs_sb[:, hp * 1024 + blk * 512:
                                               hp * 1024 + (blk + 1) * 512],
                                     in0=pb[bnk][:, 0:512], in1=rb_bc[:],
                                     op=ALU.mult),
                                 reads=[PB(bnk), krb],
                                 writes=[("Qs", blk, hp)])

                for g in range(8):
                    ln_fold_tile(xkv_d[g], g, "kv")
                    wq_head_chain(g)
                colsum(wq_sb, ncq, "ncq", "wq")
                xq_r3 = xq_d.rearrange("p (c b t) -> p c b t", c=NCH, b=2)
                for blk in range(2):
                    ln_fold_tile(xq_r3[:, :, blk, :], 8 + blk, "q")
            P.fence(skip=("pe",))
        P.fence(skip=("pe",))
        if DEBUG:
            P.op("sync", lambda eng: eng.dma_start(out=dbg_wq, in_=wq_sb[:]),
                 reads=[("wq", nn) for nn in range(NKV)],
                 writes=["dbgwq"], dma="qs")
            P.op("sync", lambda eng: eng.dma_start(out=dbg_kc, in_=kc_sb[:]),
                 reads=[("Kc", g, hp) for g in range(8) for hp in range(4)],
                 writes=["dbgkc"], dma="qs")
            P.op("sync", lambda eng: eng.dma_start(out=dbg_vc, in_=vc_sb[:]),
                 reads=[("Vc", sl) for sl in range(32)] + ["vc_ones"],
                 writes=["dbgvc"], dma="qs")
            P.op("sync", lambda eng: eng.dma_start(out=dbg_qs, in_=qs_sb[:]),
                 reads=[("Qs", b, hp) for b in range(2) for hp in range(4)],
                 writes=["dbgqs"], dma="qs")

        xres_scope = ExitStack()
        xres = xres_scope.enter_context(
            nc.sbuf_tensor("xres", [128, NCH * 1024], F32))
        xres_v = xres.ap().rearrange("p (c t) -> p c t", c=NCH)
        P.op("sync", lambda eng: eng.dma_start(out=xres[:], in_=xq_d),
             writes=["xres"], dma="qs")

        def slot_sched(blk):
            out = []
            if blk == 0:
                for s in range(12):
                    out.append((s, "data_a", s))
                for i in range(4):
                    out.append((28 + i, "tri", i))
            else:
                for s in range(24):
                    out.append((s, "data_b", s))
                for i in range(4):
                    out.append((24 + i, "tri", i))
                for s in range(28, 32):
                    out.append((s, "keep", 0))
            return out

        with ExitStack() as p2z:
            zsb = p2z.enter_context(nc.sbuf_tensor("zsb", [128, 2048], BF16))
            expT = [p2z.enter_context(
                nc.sbuf_tensor(f"expT{i}", [128, 1024], BF16))
                for i in range(2)]
            ztmp = p2z.enter_context(nc.sbuf_tensor("ztmp", [64, 512], BF16))
            rbl = p2z.enter_context(nc.sbuf_tensor("rbl", [64, 1024], BF16))
            lbuf = p2z.enter_context(nc.sbuf_tensor("lbuf", [65, 1024], F32))
            sqr = [p2z.enter_context(
                nc.sbuf_tensor(f"sqr{i}", [128, 512], BF16)) for i in range(2)]
            mb_bf = p2z.enter_context(nc.sbuf_tensor("mb_bf", [128, 512], BF16))
            rb_bf = p2z.enter_context(nc.sbuf_tensor("rb_bf", [128, 512], BF16))
            tf = p2z.enter_context(nc.sbuf_tensor("tf", [128, 512], F32))
            wob = [p2z.enter_context(
                nc.sbuf_tensor(f"wob{i}", [128, 1024], BF16)) for i in range(2)]

            def emit_attention(blk):
                sched = slot_sched(blk)
                nslots = len(sched)
                for hp in range(4):
                    def emit_sc(si):
                        s = sched[si][0]
                        g = s // 4
                        sb0 = (si % 2) * 2
                        for par in range(2):
                            lo, hi = 64 * par, 64 * par + 64
                            P.op("pe", lambda eng, hp=hp, s=s, blk=blk,
                                 lo=lo, hi=hi, sb=sb0 + par: eng.matmul(
                                     out=pb[sb][:, 0:512],
                                     lhsT=kc_sb[lo:hi, hp * S + s * 128:
                                                hp * S + (s + 1) * 128],
                                     rhs=qs_sb[lo:hi,
                                               hp * 1024 + blk * 512:
                                               hp * 1024 + (blk + 1) * 512],
                                     start=True, stop=True),
                                 reads=[("Kc", g, hp), ("Qs", blk, hp)],
                                 writes=[PB(sb0 + par)])

                    def emit_tail(si):
                        s, bk, bi = sched[si]
                        sp = si % 2
                        sb0 = sp * 2
                        if bk == "data_a":
                            bias = mba_sb[:, bi:bi + 1]
                            brd = ["mba"]
                        elif bk == "data_b":
                            bias = mbb_sb[:, bi:bi + 1]
                            brd = ["mbb"]
                        else:
                            bias = cst[:, 0:1]
                            brd = ["cst"]
                        P.op("act", lambda eng, sp=sp, bias=bias:
                             eng.activation(
                                 out=expT[sp][:], in_=pbw[sp][:],
                                 func=AF.Exp, bias=bias,
                                 scale=cst[:, 1:2]),
                             reads=[PB(2 * sp), PB(2 * sp + 1), "cst"] + brd,
                             writes=[("expT", sp)])
                        if bk == "tri":
                            c0 = 384 - bi * 128
                            for par in range(2):
                                P.op("dve", lambda eng, sp=sp, par=par,
                                     c0=c0: eng.tensor_tensor(
                                         out=expT[sp][:, par * 512:
                                                      (par + 1) * 512],
                                         in0=expT[sp][:, par * 512:
                                                      (par + 1) * 512],
                                         in1=tri[:, c0:c0 + 512],
                                         op=ALU.mult),
                                     reads=[("expT", sp), "tri"],
                                     writes=[("expT", sp)])
                        for par in range(2):
                            s_ = sched[si][0]
                            n = 2 * hp + par
                            P.op("pe", lambda eng, s=s_, n=n, par=par,
                                 sp=sp, si=si: eng.matmul(
                                     out=pb[4 + par][0:65, 0:512],
                                     lhsT=vc_sb[:, s * 520 + n * 65:
                                                s * 520 + (n + 1) * 65],
                                     rhs=expT[sp][:, par * 512:
                                                  (par + 1) * 512],
                                     start=(si == 0),
                                     stop=(si == nslots - 1),
                                     skip_group_check=True),
                                 reads=[("Vc", s), ("expT", sp), "vc_ones"],
                                 writes=[PB(4 + par)])

                    for si in range(nslots + 1):
                        if si < nslots:
                            emit_sc(si)
                        if si >= 1:
                            emit_tail(si - 1)
                    for par in range(2):
                        P.op("dve", lambda eng, par=par: eng.reciprocal(
                            out=lbuf[64:65, par * 512:(par + 1) * 512],
                            in_=pb[4 + par][64:65, 0:512]),
                            reads=[PB(4 + par)], writes=["lbuf"])
                        P.op("pe", lambda eng, par=par: eng.matmul(
                            out=pb[par][0:64, 0:512],
                            lhsT=ones65r[64:65, 0:64],
                            rhs=lbuf[64:65, par * 512:(par + 1) * 512],
                            start=True, stop=True),
                            reads=["lbuf", "ones"], writes=[PB(par)])
                        P.op("dve", lambda eng, par=par: eng.tensor_copy(
                            out=rbl[:, par * 512:(par + 1) * 512],
                            in_=pb[par][0:64, 0:512]),
                            reads=[PB(par)], writes=["rbl"])
                    P.op("dve", lambda eng, hp=hp: eng.tensor_tensor(
                        out=zsb[0:64, hp * 512:(hp + 1) * 512],
                        in0=pb[4][0:64, 0:512], in1=rbl[:, 0:512],
                        op=ALU.mult),
                        reads=[PB(4), "rbl"], writes=[("zsb", hp, 0)])
                    P.op("dve", lambda eng: eng.tensor_tensor(
                        out=ztmp[:], in0=pb[5][0:64, 0:512],
                        in1=rbl[:, 512:1024], op=ALU.mult),
                        reads=[PB(5), "rbl"], writes=["ztmp"])
                    P.op("sync", lambda eng, hp=hp: eng.dma_start(
                        out=zsb[64:128, hp * 512:(hp + 1) * 512],
                        in_=ztmp[:]),
                        reads=["ztmp"], writes=[("zsb", hp, 1)], dma="qs")

            def emit_oproj(blk, ao):
                zreads = [("zsb", hp, pp) for hp in range(4) for pp in range(2)]
                for m in range(NCH):
                    bnk = 2 + m % 2
                    mp2, mh = divmod(m, 2)
                    wb = wob[mp2 % 2]
                    if mh == 0:
                        P.op("pool", lambda eng, m=m, wb=wb: eng.dma_start(
                            out=wb.ap().rearrange("p (q d) -> p q d", q=2),
                            in_=wo_d[m:m + 2].rearrange("q p d -> p q d")),
                            writes=[("wob", mp2 % 2)], dma="qp")

                    def omm(eng, m=m, bnk=bnk, wb=wb, mh=mh):
                        r = None
                        for kcb in range(4):
                            r = eng.matmul(
                                out=pb[bnk][:, 0:512],
                                lhsT=wb[:, mh * 512 + kcb * 128:
                                        mh * 512 + (kcb + 1) * 128],
                                rhs=zsb[:, kcb * 512:(kcb + 1) * 512],
                                start=(kcb == 0), stop=(kcb == 3))
                        return r
                    P.op("pe", omm, reads=zreads + [("wob", mp2 % 2)],
                         writes=[PB(bnk)])
                    P.op("act", lambda eng, m=m, bnk=bnk, ao=ao: eng.activation(
                        out=ao[:, m * 512:(m + 1) * 512],
                        in_=pb[bnk][:, 0:512], func=AF.Copy),
                        reads=[PB(bnk)], writes=[("ao", m)])
                if DEBUG:
                    P.op("sync", lambda eng, blk=blk: eng.dma_start(
                        out=dbg_zsb[blk], in_=zsb[:]),
                        reads=zreads, writes=["dbgz"], dma="qs")
                    P.op("sync", lambda eng, blk=blk, ao=ao: eng.dma_start(
                        out=dbg_ao[blk], in_=ao[:]),
                        reads=[("ao", m) for m in range(NCH)],
                        writes=["dbgao"], dma="qs")

            def emit_postln(blk, ao):
                ln_stats_rows([(ao[:, c * 512:(c + 1) * 512], ("ao", c))
                               for c in range(NCH)],
                              [(sqr[0][:], ("sqr", 0)),
                               (sqr[1][:], ("sqr", 1))],
                              ri=blk, bx=6, bq=7)
                ln_rows_finish(ri=blk, bx=6, bq=7,
                               mb_bf=(mb_bf[:], "mb_bf"),
                               rb_bf=(rb_bf[:], "rb_bf"))
                for c in range(NCH):
                    P.op("pool", lambda eng, c=c, ao=ao: eng.tensor_tensor(
                        out=sqr[0][:], in0=ao[:, c * 512:(c + 1) * 512],
                        in1=mb_bf[:], op=ALU.subtract),
                        reads=[("ao", c), "mb_bf"], writes=[("sqr", 0)])
                    P.op("dve", lambda eng: eng.tensor_tensor(
                        out=tf[:], in0=sqr[0][:], in1=rb_bf[:],
                        op=ALU.mult),
                        reads=[("sqr", 0), "rb_bf"], writes=["tf"])
                    P.op("dve", lambda eng, c=c, blk=blk:
                         eng.tensor_tensor(
                             out=xres_v[:, c, blk * 512:(blk + 1) * 512],
                             in0=xres_v[:, c, blk * 512:(blk + 1) * 512],
                             in1=tf[:], op=ALU.add),
                         reads=["tf", "xres"], writes=["xres"])

            emit_attention(0)
            pl0 = ExitStack()
            ao0 = pl0.enter_context(
                nc.sbuf_tensor("ao_0", [128, NCH * 512], BF16))
            emit_oproj(0, ao0)
            if kernel_overlap_p2:
                emit_attention(1)
                emit_postln(0, ao0)
            else:
                P.fence()
                emit_postln(0, ao0)
                P.fence()
                emit_attention(1)
                P.fence()
            pl0.close()
            P.fence(skip=("pe",))
            with ExitStack() as pl1:
                ao1 = pl1.enter_context(
                    nc.sbuf_tensor("ao_1", [128, NCH * 512], BF16))
                emit_oproj(1, ao1)
                emit_postln(1, ao1)
        P.fence(skip=("pe",))
        if DEBUG:
            P.op("sync", lambda eng: eng.dma_start(out=dbg_xattn, in_=xres[:]),
                 reads=["xres"], writes=["dbgxa"], dma="qs")
        attn_scope.close()

        out_r = out_d.rearrange("p (c t) -> p c t", c=NCH)
        with ExitStack() as p3:
            S3 = lambda *a, **k: p3.enter_context(nc.sbuf_tensor(*a, **k))
            fuse = S3("fuse", [128, 64 * 512], BF16)
            dn = S3("dn", [128, NCH * 512], BF16)
            hln = S3("hln", [128, NCH * 512], BF16)
            wgb = [S3(f"wgb{i}", [128, 1024], BF16) for i in range(2)]
            wub = [S3(f"wub{i}", [128, 1024], BF16) for i in range(2)]
            wdb = [S3(f"wdb{i}", [128, 1024], BF16) for i in range(2)]
            sil = [S3(f"sil{i}", [128, 512], BF16) for i in range(2)]
            mb_bf3 = S3("mb_bf3", [128, 512], BF16)
            rb_bf3 = S3("rb_bf3", [128, 512], BF16)
            sqr = [S3(f"fsq{i}", [128, 512], BF16) for i in range(2)]
            tf2 = S3("tf2", [128, 512], F32)
            wd_q = wd_d.rearrange("a b p (c d) -> a b p c d", c=32)
            for blk in range(2):
                for c in range(NCH):
                    P.op("dve", lambda eng, c=c, blk=blk: eng.tensor_copy(
                        out=sil[c % 2][:],
                        in_=xres_v[:, c, blk * 512:(blk + 1) * 512]),
                        reads=["xres"], writes=[("sil", c % 2)])
                    P.op("pe", lambda eng, c=c: eng.matmul(
                        out=pb[6][0:1, 0:512], lhsT=ones_c_bf[:],
                        rhs=sil[c % 2][:], start=(c == 0), stop=(c == NCH - 1)),
                        reads=[("sil", c % 2), "ones"], writes=[PB(6)])
                    P.op("dve", lambda eng, c=c: eng.tensor_tensor(
                        out=sqr[c % 2][:], in0=sil[c % 2][:],
                        in1=sil[c % 2][:], op=ALU.mult),
                        reads=[("sil", c % 2)], writes=[("fsq", c % 2)])
                    P.op("pe", lambda eng, c=c: eng.matmul(
                        out=pb[7][0:1, 0:512], lhsT=ones_c_bf[:],
                        rhs=sqr[c % 2][:], start=(c == 0), stop=(c == NCH - 1)),
                        reads=[("fsq", c % 2), "ones"], writes=[PB(7)])
                ln_rows_finish(ri=0, bx=6, bq=7, want_mb=True)
                if DEBUG:
                    pass
                for c in range(NCH):
                    P.op("pool", lambda eng, c=c, blk=blk:
                         eng.tensor_tensor(
                             out=hln[:, c * 512:(c + 1) * 512],
                             in0=xres_v[:, c, blk * 512:(blk + 1) * 512],
                             in1=ROWS[0]["mb_bc"][:], op=ALU.subtract),
                         reads=["xres", ("mb_bc", 0)], writes=[("hln", c)])
                    P.op("dve", lambda eng, c=c: eng.tensor_tensor(
                        out=hln[:, c * 512:(c + 1) * 512],
                        in0=hln[:, c * 512:(c + 1) * 512],
                        in1=ROWS[0]["rb_bc"][:], op=ALU.mult),
                        reads=[("hln", c), ("rb_bc", 0)], writes=[("hln", c)])
                hreads = [("hln", c) for c in range(NCH)]
                if DEBUG:
                    P.op("sync", lambda eng, blk=blk: eng.dma_start(
                        out=dbg_hln[blk], in_=hln[:]),
                        reads=hreads, writes=["dbghln"], dma="qs")
                for ic in range(64):
                    for h in range(2):
                        bi = (ic * 2 + h) % 2
                        P.op("pool", lambda eng, bi=bi, ic=ic, h=h:
                             eng.dma_start(
                                 out=wgb[bi][:], in_=wg_d[ic, h]),
                             writes=[("wgb", bi)], dma="qp")
                        P.op("pool", lambda eng, bi=bi, ic=ic, h=h:
                             eng.dma_start(
                                 out=wub[bi][:], in_=wu_d[ic, h]),
                             writes=[("wub", bi)], dma="qp")

                        def gmm(eng, bi=bi, h=h):
                            r = None
                            for c in range(8):
                                r = eng.matmul(
                                    out=pb[0][:, 0:512],
                                    lhsT=wgb[bi][:, c * 128:(c + 1) * 128],
                                    rhs=hln[:, (h * 8 + c) * 512:
                                             (h * 8 + c + 1) * 512],
                                    start=(h == 0 and c == 0),
                                    stop=(h == 1 and c == 7),
                                    skip_group_check=True)
                            return r
                        P.op("pe", gmm, reads=hreads + [("wgb", bi)],
                             writes=[PB(0)])

                        def umm(eng, bi=bi, h=h):
                            r = None
                            for c in range(8):
                                r = eng.matmul(
                                    out=pb[1][:, 0:512],
                                    lhsT=wub[bi][:, c * 128:(c + 1) * 128],
                                    rhs=hln[:, (h * 8 + c) * 512:
                                             (h * 8 + c + 1) * 512],
                                    start=(h == 0 and c == 0),
                                    stop=(h == 1 and c == 7),
                                    skip_group_check=True)
                            return r
                        P.op("pe", umm, reads=hreads + [("wub", bi)],
                             writes=[PB(1)])
                    P.op("act", lambda eng, ic=ic: eng.activation(
                        out=sil[ic % 2][:], in_=pb[0][:, 0:512],
                        func=AF.Silu, scale=sg_sb[:, ic:ic + 1]),
                        reads=[PB(0), ("sg", ic)],
                        writes=[("sil", ic % 2)])
                    P.op("dve", lambda eng, ic=ic:
                         eng.scalar_tensor_tensor(
                             out=fuse[:, ic * 512:(ic + 1) * 512],
                             in0=pb[1][:, 0:512],
                             scalar=su_sb[:, ic:ic + 1],
                             in1=sil[ic % 2][:], op0=ALU.mult,
                             op1=ALU.mult),
                         reads=[PB(1), ("sil", ic % 2), ("su", ic)],
                         writes=[("fuse", ic)])
                freads = [("fuse", ic) for ic in range(64)]
                for mp in range(8):
                    for q16 in range(16):
                        bi = q16 % 2
                        cih, half = divmod(q16, 8)
                        P.op("pool", lambda eng, bi=bi, mp=mp, cih=cih,
                             half=half: eng.dma_start(
                                 out=wdb[bi][:],
                                 in_=wd_q[mp, cih, :,
                                          half * 4:(half + 1) * 4, :]),
                             writes=[("wdb", bi)], dma="qp")
                        for dm in range(2):
                            m = mp * 2 + dm

                            def dmm(eng, bi=bi, q16=q16, dm=dm):
                                r = None
                                for cc in range(4):
                                    r = eng.matmul(
                                        out=pb[2 + dm][:, 0:512],
                                        lhsT=wdb[bi][:,
                                                     cc * 256 + dm * 128:
                                                     cc * 256 + (dm + 1) * 128],
                                        rhs=fuse[:,
                                                 (q16 * 4 + cc) * 512:
                                                 (q16 * 4 + cc + 1) * 512],
                                        start=(q16 == 0 and cc == 0),
                                        stop=(q16 == 15 and cc == 3),
                                        skip_group_check=True)
                                return r
                            P.op("pe", dmm,
                                 reads=freads + [("wdb", bi)],
                                 writes=[PB(2 + dm)])
                            if q16 == 15:
                                P.op("act", lambda eng, m=m, dm=dm:
                                     eng.activation(
                                         out=dn[:, m * 512:(m + 1) * 512],
                                         in_=pb[2 + dm][:, 0:512],
                                         func=AF.Copy,
                                         scale=sd_sb[:, m:m + 1]),
                                     reads=[PB(2 + dm), ("sd", m)],
                                     writes=[("dn", m)])
                if DEBUG:
                    P.op("sync", lambda eng, blk=blk: eng.dma_start(
                        out=dbg_dn[blk], in_=dn[:]),
                        reads=[("dn", c) for c in range(NCH)],
                        writes=["dbgdn"], dma="qs")
                ln_stats_rows([(dn[:, c * 512:(c + 1) * 512], ("dn", c))
                               for c in range(NCH)],
                              [(sqr[0][:], ("fsq", 0)),
                               (sqr[1][:], ("fsq", 1))],
                              ri=1, bx=4, bq=5)
                ln_rows_finish(ri=1, bx=4, bq=5,
                               mb_bf=(mb_bf3[:], "mb_bf3"),
                               rb_bf=(rb_bf3[:], "rb_bf3"))
                for c in range(NCH):
                    P.op("pool", lambda eng, c=c: eng.tensor_tensor(
                        out=sqr[0][:], in0=dn[:, c * 512:(c + 1) * 512],
                        in1=mb_bf3[:], op=ALU.subtract),
                        reads=[("dn", c), "mb_bf3"], writes=[("fsq", 0)])
                    P.op("dve", lambda eng: eng.tensor_tensor(
                        out=tf2[:], in0=sqr[0][:], in1=rb_bf3[:],
                        op=ALU.mult),
                        reads=[("fsq", 0), "rb_bf3"], writes=["tf2"])
                    P.op("dve", lambda eng, c=c, blk=blk:
                         eng.tensor_tensor(
                             out=xres_v[:, c, blk * 512:(blk + 1) * 512],
                             in0=xres_v[:, c, blk * 512:(blk + 1) * 512],
                             in1=tf2[:], op=ALU.add),
                         reads=["tf2", "xres"],
                         writes=["xres", ("xout", blk, c)])
                P.op("sync", lambda eng, blk=blk: eng.dma_start(
                    out=out_r[:, :, blk * 512:(blk + 1) * 512],
                    in_=xres_v[:, :, blk * 512:(blk + 1) * 512]),
                    reads=[("xout", blk, c) for c in range(NCH)],
                    writes=[("outd", blk)], dma="qs")
        xres_scope.close()

        P.tail_wait("sync", ["qs", "qp"])

        with nc.Block() as block:
            def run_stream(name):
                def _(eng):
                    for waits, emit, inc in P.ops[name]:
                        for (s, v) in waits:
                            eng.wait_ge(sem[s], v)
                        if emit is not None:
                            inst = emit(eng)
                            if inst is None:
                                raise RuntimeError(f"op on {name} returned None")
                            inst.then_inc(sem[inc[0]], inc[1])
                return _

            block.tensor(run_stream("pe"))
            block.vector(run_stream("dve"))
            block.scalar(run_stream("act"))
            block.gpsimd(run_stream("pool"))
            block.sync(run_stream("sync"))

    return nc


_NC_CACHE = None


def _get_nc():
    global _NC_CACHE
    if _NC_CACHE is None:
        _NC_CACHE = build_program()
    return _NC_CACHE


def kernel(L0_attn_normalized_resid_pre, L0_attn_W_Q, L0_attn_W_K, L0_attn_W_V,
           L0_attn_W_O, L0_ffn_w_gate, L0_ffn_w_up, L0_ffn_w_down,
           L0_ffn_s_gate, L0_ffn_s_up, L0_ffn_s_down):
    x = np.asarray(L0_attn_normalized_resid_pre)
    xT = np.ascontiguousarray(x.transpose(0, 2, 1))
    _t = lambda w, h: np.ascontiguousarray(np.asarray(w, dtype=np.float32).reshape(
        h, 16, 128, 64).transpose(0, 2, 1, 3).reshape(h, 128, 1024))
    wq = _t(L0_attn_W_Q, 32)
    wk = _t(L0_attn_W_K, 8)
    wv = _t(L0_attn_W_V, 8)
    wo = np.ascontiguousarray(np.asarray(L0_attn_W_O, dtype=np.float32)
                              .reshape(4, 128, 16, 128)
                              .transpose(2, 1, 0, 3).reshape(16, 128, 512))
    _ffn_tile = lambda w: np.ascontiguousarray(np.asarray(w).reshape(
        2, 8, 128, 64, 128).transpose(3, 0, 2, 1, 4).reshape(64, 2, 128, 1024))
    wg = _ffn_tile(L0_ffn_w_gate)
    wu = _ffn_tile(L0_ffn_w_up)
    wd = np.ascontiguousarray(np.asarray(L0_ffn_w_down).reshape(
        2, 32, 128, 8, 256).transpose(3, 0, 2, 1, 4).reshape(8, 2, 128, 8192))
    sg = np.ascontiguousarray(np.asarray(L0_ffn_s_gate), dtype=np.float32)
    su = np.ascontiguousarray(np.asarray(L0_ffn_s_up), dtype=np.float32)
    sd = np.ascontiguousarray(np.asarray(L0_ffn_s_down), dtype=np.float32)

    in_maps = []
    metas = []
    for core in range(8):
        s = core // 4
        j = core % 4
        A, B = j, 7 - j
        xq0 = np.concatenate(
            [xT[s][:, A * 512:(A + 1) * 512], xT[s][:, B * 512:(B + 1) * 512]],
            axis=1)
        xq = np.ascontiguousarray(
            xq0.reshape(16, 128, 2, 512).transpose(1, 0, 2, 3).reshape(128, 16384))
        others = [b for b in range(8) if b not in (A, B)]
        order = [b for b in others if b < A] + [b for b in others if b >= A]
        cols = [xT[s][:, b * 512:(b + 1) * 512] for b in order]
        cols.append(xT[s][:, B * 512:(B + 1) * 512])
        cols.append(xT[s][:, A * 512:(A + 1) * 512])
        xkv0 = np.concatenate(cols, axis=1)
        xkv = np.ascontiguousarray(
            xkv0.reshape(16, 128, 8, 512).transpose(2, 1, 0, 3).reshape(8, 128, 8192))
        mba = np.full((128, 12), ESHIFT - 10000.0, np.float32)
        mbb = np.full((128, 24), ESHIFT - 10000.0, np.float32)
        for t in range(12):
            if order[t // 4] < A:
                mba[:, t] = ESHIFT
        for t in range(24):
            if order[t // 4] < B:
                mbb[:, t] = ESHIFT
        in_maps.append({
            "xq": xq, "xkv": xkv, "wq": wq, "wk": wk, "wv": wv, "wo": wo,
            "wg": wg, "wu": wu, "wd": wd, "sg": sg, "su": su, "sd": sd,
            "mba": mba, "mbb": mbb,
        })
        metas.append((s, A, B))

    nc = _get_nc()
    res = run_bass_kernel_spmd(nc, in_maps, core_ids=list(range(8)))
    global LAST_RESULTS
    LAST_RESULTS = res if DEBUG else None

    outT = np.empty((2, 2048, 4096), np.float32)
    for core, (s, A, B) in enumerate(metas):
        o = np.asarray(res.results[core]["out"]).reshape(128, 16, 2, 512)
        outT[s][:, A * 512:(A + 1) * 512] = \
            o[:, :, 0, :].transpose(1, 0, 2).reshape(2048, 512)
        outT[s][:, B * 512:(B + 1) * 512] = \
            o[:, :, 1, :].transpose(1, 0, 2).reshape(2048, 512)
    return np.ascontiguousarray(outT.transpose(0, 2, 1))


# version 13
# speedup vs baseline: 1.0299x; 1.0299x over previous
import numpy as np
from contextlib import ExitStack

import concourse.bass as bass
import concourse.mybir as mybir
from concourse.bass_utils import run_bass_kernel_spmd

F32 = mybir.dt.float32
BF16 = mybir.dt.bfloat16
I8 = mybir.dt.int8
AF = mybir.ActivationFunctionType
ALU = mybir.AluOpType

D = 2048
S = 4096
NKV = 8
INTER = 8192
NCH = D // 128
EPS = 1e-5
ESHIFT = -30.0

ENGINES = ("pe", "dve", "act", "pool", "sync")


class Prog:

    def __init__(self):
        self.ops = {e: [] for e in ENGINES}
        self.cnt = {"pe": 0, "dve": 0, "act": 0, "pool": 0, "qs": 0, "qp": 0}
        self.last_w = {}
        self.readers = {}
        self.waited = {e: {} for e in ENGINES}

    def op(self, engine, emit, reads=(), writes=(), dma=None):
        rd, wr = [], list(writes)
        for k in reads:
            if isinstance(k, tuple) and k and k[0] == "pb":
                wr.append(k)
            else:
                rd.append(k)
        need = {}

        def _need(sem, val):
            if val > need.get(sem, 0):
                need[sem] = val

        for k in rd:
            w = self.last_w.get(k)
            if w:
                _need(*w)
        for k in wr:
            w = self.last_w.get(k)
            if w:
                _need(*w)
            for (rsem, rcnt) in self.readers.get(k, ()):
                if dma is None and rsem == engine:
                    continue
                _need(rsem, rcnt)
        waits = []
        for semn, val in need.items():
            if self.waited[engine].get(semn, 0) < val:
                waits.append((semn, val))
                self.waited[engine][semn] = val
        if dma:
            self.cnt[dma] += 16
            comp = (dma, self.cnt[dma])
            inc = (dma, 16)
        else:
            self.cnt[engine] += 1
            comp = (engine, self.cnt[engine])
            inc = (engine, 1)
        for k in wr:
            self.last_w[k] = comp
            self.readers[k] = []
        for k in rd:
            self.readers.setdefault(k, []).append(comp)
        self.ops[engine].append((waits, emit, inc))

    def fence(self, skip=()):
        for e in ENGINES:
            if e in skip:
                continue
            waits = []
            for semn, val in self.cnt.items():
                if val > 0 and self.waited[e].get(semn, 0) < val:
                    waits.append((semn, val))
                    self.waited[e][semn] = val
            if waits:
                self.ops[e].append((waits, None, None))

    def tail_wait(self, engine, sems):
        waits = []
        for semn in sems:
            val = self.cnt[semn]
            if val > 0 and self.waited[engine].get(semn, 0) < val:
                waits.append((semn, val))
                self.waited[engine][semn] = val
        if waits:
            self.ops[engine].append((waits, None, None))


DEBUG = False
LAST_RESULTS = None
kernel_overlap_p2 = True


def build_program():
    nc = bass.Bass("TRN2", target_bir_lowering=False)
    P = Prog()

    xq_d = nc.dram_tensor("xq", [128, NCH * 1024], F32, kind="ExternalInput").ap()
    xkv_d = nc.dram_tensor("xkv", [8, 128, NCH * 512], F32, kind="ExternalInput").ap()
    wq_d = nc.dram_tensor("wq", [32, 128, 1024], F32, kind="ExternalInput").ap()
    wk_d = nc.dram_tensor("wk", [NKV, 128, 1024], F32, kind="ExternalInput").ap()
    wv_d = nc.dram_tensor("wv", [NKV, 128, 1024], F32, kind="ExternalInput").ap()
    wo_d = nc.dram_tensor("wo", [16, 128, 512], F32, kind="ExternalInput").ap()
    wg_d = nc.dram_tensor("wg", [64, 2, 128, 1024], I8, kind="ExternalInput").ap()
    wu_d = nc.dram_tensor("wu", [64, 2, 128, 1024], I8, kind="ExternalInput").ap()
    wd_d = nc.dram_tensor("wd", [8, 2, 128, 32 * 256], I8, kind="ExternalInput").ap()
    sg_d = nc.dram_tensor("sg", [INTER], F32, kind="ExternalInput").ap()
    su_d = nc.dram_tensor("su", [INTER], F32, kind="ExternalInput").ap()
    sd_d = nc.dram_tensor("sd", [D], F32, kind="ExternalInput").ap()
    mba_d = nc.dram_tensor("mba", [128, 12], F32, kind="ExternalInput").ap()
    mbb_d = nc.dram_tensor("mbb", [128, 24], F32, kind="ExternalInput").ap()
    out_d = nc.dram_tensor("out", [128, NCH * 1024], F32, kind="ExternalOutput").ap()
    if DEBUG:
        dbg_kc = nc.dram_tensor("dbg_kc", [128, 4 * S], BF16,
                                kind="ExternalOutput").ap()
        dbg_vc = nc.dram_tensor("dbg_vc", [128, 32 * 520], BF16,
                                kind="ExternalOutput").ap()
        dbg_qs = nc.dram_tensor("dbg_qs", [128, 4 * 1024], BF16,
                                kind="ExternalOutput").ap()
        dbg_zsb = nc.dram_tensor("dbg_zsb", [2, 128, 2048], BF16,
                                 kind="ExternalOutput").ap()
        dbg_ao = nc.dram_tensor("dbg_ao", [2, 128, NCH * 512], BF16,
                                kind="ExternalOutput").ap()
        dbg_xattn = nc.dram_tensor("dbg_xattn", [128, NCH * 1024], F32,
                                   kind="ExternalOutput").ap()
        dbg_wq = nc.dram_tensor("dbg_wq", [128, NCH * 512], BF16,
                                kind="ExternalOutput").ap()
        dbg_hln = nc.dram_tensor("dbg_hln", [2, 128, NCH * 512], BF16,
                                 kind="ExternalOutput").ap()
        dbg_dn = nc.dram_tensor("dbg_dn", [2, 128, NCH * 512], BF16,
                                kind="ExternalOutput").ap()

    with ExitStack() as top:
        SB = lambda *a, **k: top.enter_context(nc.sbuf_tensor(*a, **k))
        pbw = [top.enter_context(nc.psum_tensor(f"pbw{i}", [128, 1024], F32))
               for i in range(2)]
        pbs = [top.enter_context(nc.psum_tensor(f"pb{i}", [128, 512], F32))
               for i in range(4, 8)]
        pb = {}
        for i in range(4):
            pb[i] = pbw[i // 2].ap()[:, (i % 2) * 1024 // 2:(i % 2) * 512 + 512]
        for i in range(4):
            pb[4 + i] = pbs[i].ap()
        PB = lambda i: ("pb", i)
        sem = {name: top.enter_context(nc.semaphore(f"sem_{name}"))
               for name in ("pe", "dve", "act", "pool", "qs", "qp")}

        tri = SB("tri", [128, 896], BF16)
        ones_c_bf = SB("ones_c_bf", [128, 1], BF16)
        ones_r_f = SB("ones_r_f", [1, 128], F32)
        ones65r = SB("ones65r", [65, 64], F32)
        sg_sb = SB("sg_sb", [128, 64], F32)
        su_sb = SB("su_sb", [128, 64], F32)
        sd_sb = SB("sd_sb", [128, 16], F32)
        mba_sb = SB("mba_sb", [128, 12], F32)
        mbb_sb = SB("mbb_sb", [128, 24], F32)
        m2shared = SB("m2shared", [1, 512], F32)
        ROWS = []
        for ri in range(2):
            ROWS.append(dict(
                mrow_f=SB(f"mrow_f{ri}", [1, 512], F32),
                qrow_f=SB(f"qrow_f{ri}", [1, 512], F32),
                m2row=None,
                rrow_f=SB(f"rrow_f{ri}", [1, 512], F32),
                mrow_bf=SB(f"mrow_bf{ri}", [1, 512], BF16),
                rcol=SB(f"rcol{ri}", [128, 4], F32),
                rb_bc=SB(f"rb_bc{ri}", [128, 512], F32),
                mb_bc=SB(f"mb_bc{ri}", [128, 512], F32),
            ))
        cst = SB("cst", [128, 8], F32)
        CST_ESHIFT, CST_SC8, CST_INVD, CST_EPS, CST_NEG1 = 0, 1, 2, 3, 4

        attn_scope = ExitStack()
        SR = lambda *a, **k: attn_scope.enter_context(
            nc.sbuf_tensor(*a, side="right", **k))
        kc_sb = SR("kc_sb", [128, 4 * S], BF16)
        vc_sb = SR("vc_sb", [128, 32 * 520], BF16)
        qs_sb = SR("qs_sb", [128, 4 * 1024], BF16)

        def E(fn):
            return fn

        def ln_stats_rows(xsrc, sqbufs, ri=0, bx=6, bq=7):
            n = len(xsrc)
            for c in range(n):
                ap_c, k_c = xsrc[c]
                sq_ap, sq_key = sqbufs[c % 2]
                P.op("pe", lambda eng, ap=ap_c, c=c, n=n: eng.matmul(
                    out=pb[bx][0:1, 0:512], lhsT=ones_c_bf[:], rhs=ap,
                    start=(c == 0), stop=(c == n - 1)),
                    reads=[k_c, "ones"], writes=[PB(bx)])
                P.op("dve", lambda eng, ap=ap_c, sq=sq_ap: eng.tensor_tensor(
                    out=sq, in0=ap, in1=ap, op=ALU.mult),
                    reads=[k_c], writes=[sq_key])
                P.op("pe", lambda eng, sq=sq_ap, c=c, n=n: eng.matmul(
                    out=pb[bq][0:1, 0:512], lhsT=ones_c_bf[:], rhs=sq,
                    start=(c == 0), stop=(c == n - 1)),
                    reads=[sq_key, "ones"], writes=[PB(bq)])

        def ln_rows_finish(ri=0, bx=6, bq=7, want_bf=False, want_col=False,
                           want_mb=False, mb_bf=None, rb_bf=None):
            R = ROWS[ri]
            mrow_f, qrow_f, m2row = R["mrow_f"], R["qrow_f"], m2shared
            rrow_f, mrow_bf = R["rrow_f"], R["mrow_bf"]
            rcol, rb_bc, mb_bc = R["rcol"], R["rb_bc"], R["mb_bc"]
            K = lambda nm: (nm, ri) if nm != "m2row" else "m2row"
            P.op("act", lambda eng: eng.activation(
                out=mrow_f[:], in_=pb[bx][0:1, 0:512], func=AF.Copy,
                scale=cst[0:1, 2:3]), reads=[PB(bx), "cst"],
                writes=[K("mrow_f")])
            P.op("act", lambda eng: eng.activation(
                out=qrow_f[:], in_=pb[bq][0:1, 0:512], func=AF.Copy,
                scale=cst[0:1, 2:3]), reads=[PB(bq), "cst"],
                writes=[K("qrow_f")])
            P.op("dve", lambda eng: eng.tensor_tensor(
                out=m2row[:], in0=mrow_f[:], in1=mrow_f[:], op=ALU.mult),
                reads=[K("mrow_f")], writes=[K("m2row")])
            P.op("dve", lambda eng: eng.tensor_tensor(
                out=qrow_f[:], in0=qrow_f[:], in1=m2row[:], op=ALU.subtract),
                reads=[K("qrow_f"), K("m2row")], writes=[K("qrow_f")])
            P.op("act", lambda eng: eng.activation(
                out=m2row[:], in_=qrow_f[:], func=AF.Sqrt,
                bias=cst[0:1, 3:4]),
                reads=[K("qrow_f"), "cst"], writes=[K("m2row")])
            P.op("dve", lambda eng: eng.reciprocal(out=rrow_f[:], in_=m2row[:]),
                reads=[K("m2row")], writes=[K("rrow_f")])
            if want_bf:
                P.op("dve", lambda eng: eng.tensor_copy(
                    out=mrow_bf[:], in_=mrow_f[:]),
                    reads=[K("mrow_f")], writes=[K("mrow_bf")])
            P.op("pe", lambda eng: eng.matmul(
                out=pb[bx][:, 0:512], lhsT=ones_r_f[:], rhs=rrow_f[:],
                start=True, stop=True), reads=[K("rrow_f"), "ones"],
                writes=[PB(bx)])
            P.op("act", lambda eng: eng.activation(
                out=rb_bc[:], in_=pb[bx][:, 0:512], func=AF.Copy),
                reads=[PB(bx)], writes=[K("rb_bc")])
            if rb_bf is not None:
                ap, key = rb_bf
                P.op("dve", lambda eng, ap=ap: eng.tensor_copy(
                    out=ap, in_=rb_bc[:]), reads=[K("rb_bc")], writes=[key])
            if want_mb or mb_bf is not None:
                P.op("pe", lambda eng: eng.matmul(
                    out=pb[bx][:, 0:512], lhsT=ones_r_f[:], rhs=mrow_f[:],
                    start=True, stop=True), reads=[K("mrow_f"), "ones"],
                    writes=[PB(bx)])
                P.op("act", lambda eng: eng.activation(
                    out=mb_bc[:], in_=pb[bx][:, 0:512], func=AF.Copy),
                    reads=[PB(bx)], writes=[K("mb_bc")])
                if mb_bf is not None:
                    ap, key = mb_bf
                    P.op("dve", lambda eng, ap=ap: eng.tensor_copy(
                        out=ap, in_=mb_bc[:]), reads=[K("mb_bc")], writes=[key])
            if want_col:
                for ts in range(4):
                    P.op("pe", lambda eng, ts=ts: eng.transpose(
                        out=pb[bx][0:128, ts:ts + 1],
                        in_=rrow_f[0:1, ts * 128:(ts + 1) * 128],
                        identity=ones_r_f[0:1, 0:1]),
                        reads=[K("rrow_f"), "ones"], writes=[PB(bx)])
                P.op("dve", lambda eng: eng.tensor_copy(
                    out=rcol[:], in_=pb[bx][0:128, 0:4]),
                    reads=[PB(bx)], writes=[K("rcol")])

        P.op("sync", lambda eng: eng.dma_start(out=mba_sb[:], in_=mba_d),
             writes=["mba"], dma="qs")
        P.op("sync", lambda eng: eng.dma_start(out=mbb_sb[:], in_=mbb_d),
             writes=["mbb"], dma="qs")
        for (dst, src, n, nm) in ((sg_sb, sg_d, 64, "sg"), (su_sb, su_d, 64, "su"),
                                  (sd_sb, sd_d, 16, "sd")):
            for c in range(n):
                src_ap = bass.AP(src.tensor, c * 128, [[1, 128], [1, 1]])
                P.op("sync", lambda eng, d=dst, c=c, s=src_ap: eng.dma_start(
                    out=d[:, c:c + 1], in_=s), writes=[(nm, c)], dma="qs")
        with ExitStack() as msk_scope:
            ncq = msk_scope.enter_context(nc.sbuf_tensor("ncq", [1, 512], BF16))
            nck = msk_scope.enter_context(nc.sbuf_tensor("nck", [1, 512], BF16))
            ncv = msk_scope.enter_context(nc.sbuf_tensor("ncv", [1, 512], BF16))
            wt = [msk_scope.enter_context(
                nc.sbuf_tensor(f"wt{i}", [128, 512], F32)) for i in range(4)]
            iden_f = msk_scope.enter_context(
                nc.sbuf_tensor("iden_f", [128, 128], F32))
            tmp_scope = ExitStack()
            itmp = tmp_scope.enter_context(
                nc.sbuf_tensor("itmp", [128, 896], F32))
            P.op("pool", lambda eng: eng.iota(
                itmp[:, 0:896], [[1, 896]], base=0, channel_multiplier=-1,
                allow_small_or_imprecise_dtypes=True), writes=["itmp"])
            P.op("pool", lambda eng: eng.tensor_scalar(
                out=tri[:], in0=itmp[:, 0:896], scalar1=384.0, scalar2=None,
                op0=ALU.is_ge), reads=["itmp"], writes=["tri"])
            P.op("pool", lambda eng: eng.iota(
                itmp[:, 0:128], [[1, 128]], base=0, channel_multiplier=-1,
                allow_small_or_imprecise_dtypes=True), writes=["itmp"])
            P.op("pool", lambda eng: eng.tensor_scalar(
                out=iden_f[:], in0=itmp[:, 0:128], scalar1=0.0, scalar2=None,
                op0=ALU.is_equal), reads=["itmp"], writes=["iden_f"])
            P.op("dve", lambda eng: eng.memset(ones_c_bf[:], 1.0),
                 writes=["ones"])
            P.op("dve", lambda eng: eng.memset(ones_r_f[:], 1.0),
                 writes=["ones"])
            P.op("dve", lambda eng: eng.memset(ones65r[64:65, :], 1.0),
                 writes=["ones"])
            for col, val in ((0, ESHIFT), (1, 0.125), (2, 1.0 / D),
                             (3, EPS), (4, -1.0)):
                P.op("dve", lambda eng, col=col, val=val: eng.memset(
                    cst[:, col:col + 1], val), writes=["cst"])
            P.op("dve", lambda eng: eng.memset(vc_sb[:], 1.0),
                writes=["vc_ones"])

            wq_sb = msk_scope.enter_context(
                nc.sbuf_tensor("wq_sb", [128, NCH * 512], BF16, side="right"))
            wk_sb = msk_scope.enter_context(
                nc.sbuf_tensor("wk_sb", [128, NCH * 512], BF16, side="right"))
            wv_sb = msk_scope.enter_context(
                nc.sbuf_tensor("wv_sb", [128, NCH * 512], BF16, side="right"))
            for n in range(NKV):
                for (wsb, wd_, nm) in ((wk_sb, wk_d, "wkk"), (wv_sb, wv_d, "wvv")):
                    dstv = wsb.ap().rearrange(
                        "p (c m) -> p c m", c=NCH)[:, :, n * 64:(n + 1) * 64]
                    P.op("pool", lambda eng, d=dstv, wd_=wd_, n=n: eng.dma_start(
                        out=d, in_=wd_[n].rearrange("p (c k) -> p c k", c=NCH)),
                        writes=[(nm, n)], dma="qp")
            def wq_head_chain(n):
                for half in range(2):
                    for g in range(4):
                        P.op("sync", lambda eng, n=n, g=g, half=half:
                             eng.dma_start(
                                 out=wt[g][:],
                                 in_=wq_d[4 * n + g][:, half * 512:
                                                     (half + 1) * 512]),
                             writes=[("wt", g)], dma="qs")

                    def smm(eng):
                        r = None
                        for g in range(4):
                            r = eng.matmul(
                                out=pb[6][:, 0:512], lhsT=iden_f[:],
                                rhs=wt[g][:],
                                start=(g == 0), stop=(g == 3))
                        return r
                    P.op("pe", smm,
                         reads=[("wt", g) for g in range(4)] + ["iden_f"],
                         writes=[PB(6)])
                    ws, wsk = vstage[half], ("vst", half)
                    P.op("dve", lambda eng, ws=ws: eng.tensor_copy(
                        out=ws[:], in_=pb[6][:, 0:512]),
                        reads=[PB(6)], writes=[wsk])
                    dst = wq_sb.ap().rearrange(
                        "p (c m) -> p c m",
                        c=NCH)[:, half * 8:(half + 1) * 8, n * 64:(n + 1) * 64]
                    P.op("sync", lambda eng, d=dst, ws=ws: eng.dma_start(
                        out=d, in_=ws.ap().rearrange("p (c k) -> p c k", c=8)),
                        reads=[wsk], writes=[("wq", n)], dma="qs")
            def colsum(wsb, dstn, nm, wkey):
                for c in range(NCH):
                    P.op("pe", lambda eng, wsb=wsb, c=c: eng.matmul(
                        out=pb[7][0:1, 0:512], lhsT=ones_c_bf[:],
                        rhs=wsb[:, c * 512:(c + 1) * 512],
                        start=(c == 0), stop=(c == NCH - 1)),
                        reads=[(wkey, nn) for nn in range(NKV)] + ["ones"],
                        writes=[PB(7)])
                P.op("act", lambda eng, dstn=dstn: eng.activation(
                    out=dstn[:], in_=pb[7][0:1, 0:512], func=AF.Copy,
                    scale=cst[0:1, 4:5]), reads=[PB(7), "cst"], writes=[nm])

            colsum(wk_sb, nck, "nck", "wkk")
            colsum(wv_sb, ncv, "ncv", "wvv")

            P.fence(skip=("pe",))
            tmp_scope.close()
            with ExitStack() as p1:
                S1 = lambda *a, **k: p1.enter_context(nc.sbuf_tensor(*a, **k))
                xb = [S1(f"xb{i}", [128, NCH * 512], BF16) for i in range(2)]
                sqb = [S1(f"sqb{i}", [128, 512], BF16) for i in range(2)]
                vstage = [S1(f"vstage{i}", [128, 512], BF16) for i in range(2)]

                def ln_fold_tile(src_ap, tile_idx, kind):
                    bi = tile_idx % 2
                    ri = tile_idx % 2
                    bx, bq = (6, 7) if ri == 0 else (4, 5)
                    R = ROWS[ri]
                    mrow_bf, rb_bc, rcol = R["mrow_bf"], R["rb_bc"], R["rcol"]
                    kmb, krb, krc = ("mrow_bf", ri), ("rb_bc", ri), ("rcol", ri)
                    xbt, xbk = xb[bi], ("xb", bi)
                    P.op("pool", lambda eng: eng.dma_start(
                        out=xbt[:], in_=src_ap), writes=[xbk], dma="qp")
                    ln_stats_rows([(xbt[:, c * 512:(c + 1) * 512], xbk)
                                   for c in range(NCH)],
                                  [(sqb[ri][:], ("sq", ri)),
                                   (vstage[ri][:], ("vst", ri))],
                                  ri=ri, bx=bx, bq=bq)

                    def chunks_k(hp, bnk, wsb, wkey):
                        def kmm(eng, hp=hp, bnk=bnk):
                            r = None
                            for c in range(NCH):
                                r = eng.matmul(
                                    out=pb[bnk][:, 0:512],
                                    lhsT=wsb[:, c * 512 + hp * 128:
                                             c * 512 + (hp + 1) * 128],
                                    rhs=xbt[:, c * 512:(c + 1) * 512],
                                    start=(c == 0), stop=False)
                            return r
                        P.op("pe", kmm,
                             reads=[xbk] + [(wkey, nn) for nn in range(NKV)],
                             writes=[PB(bnk)])

                    def finish_k(hp, bnk, ncs, nkey, cache_write):
                        P.op("pe", lambda eng, hp=hp, bnk=bnk: eng.matmul(
                            out=pb[bnk][:, 0:512],
                            lhsT=ncs[0:1, hp * 128:(hp + 1) * 128],
                            rhs=mrow_bf[:], start=False, stop=True,
                            skip_group_check=True),
                            reads=[kmb, nkey], writes=[PB(bnk)])
                        cache_write(hp, bnk)

                    def chunks_v(ts, bnk):
                        def vmm(eng, ts=ts, bnk=bnk):
                            r = None
                            for c in range(NCH):
                                r = eng.matmul(
                                    out=pb[bnk][:, 0:512],
                                    lhsT=xbt[:, c * 512 + ts * 128:
                                             c * 512 + (ts + 1) * 128],
                                    rhs=wv_sb[:, c * 512:(c + 1) * 512],
                                    start=(c == 0), stop=False)
                            return r
                        P.op("pe", vmm,
                             reads=[xbk] + [("wvv", nn) for nn in range(NKV)],
                             writes=[PB(bnk)])

                    def finish_v(ts, bnk, g):
                        P.op("pe", lambda eng, ts=ts, bnk=bnk: eng.matmul(
                            out=pb[bnk][:, 0:512],
                            lhsT=mrow_bf[0:1, ts * 128:(ts + 1) * 128],
                            rhs=ncv[:], start=False, stop=True,
                            skip_group_check=True),
                            reads=[kmb, "ncv"], writes=[PB(bnk)])
                        slot = g * 4 + ts
                        vs, vsk = vstage[ts % 2], ("vst", ts % 2)
                        P.op("dve", lambda eng, vs=vs, ts=ts, bnk=bnk:
                             eng.tensor_scalar(
                                 out=vs[:], in0=pb[bnk][:, 0:512],
                                 scalar1=rcol[:, ts:ts + 1], scalar2=None,
                                 op0=ALU.mult),
                             reads=[PB(bnk), krc], writes=[vsk])
                        vdst = vc_sb.ap()[:, slot * 520:(slot + 1) * 520] \
                            .rearrange("p (n h) -> p n h", n=8)[:, :, 0:64]
                        P.op("sync", lambda eng, vdst=vdst, vs=vs:
                             eng.dma_start(
                                 out=vdst,
                                 in_=vs.ap().rearrange("p (n h) -> p n h",
                                                       n=8)),
                             reads=[vsk, "vc_ones"],
                             writes=[("Vc", slot)], dma="qs")

                    if kind == "kv":
                        g = tile_idx

                        def kev(hp, bnk):
                            P.op("dve", lambda eng, hp=hp, bnk=bnk:
                                 eng.tensor_tensor(
                                     out=kc_sb[:, hp * S + g * 512:
                                               hp * S + (g + 1) * 512],
                                     in0=pb[bnk][:, 0:512], in1=rb_bc[:],
                                     op=ALU.mult),
                                 reads=[PB(bnk), krb],
                                 writes=[("Kc", g, hp)])
                        chunks_k(0, 0, wk_sb, "wkk")
                        chunks_k(1, 1, wk_sb, "wkk")
                        chunks_v(0, 2)
                        chunks_v(1, 3)
                        ln_rows_finish(ri=ri, bx=bx, bq=bq, want_bf=True,
                                       want_col=True)
                        finish_k(0, 0, nck, "nck", kev)
                        finish_k(1, 1, nck, "nck", kev)
                        finish_v(0, 2, g)
                        finish_v(1, 3, g)
                        chunks_k(2, 0, wk_sb, "wkk")
                        finish_k(2, 0, nck, "nck", kev)
                        chunks_k(3, 1, wk_sb, "wkk")
                        finish_k(3, 1, nck, "nck", kev)
                        chunks_v(2, 2)
                        finish_v(2, 2, g)
                        chunks_v(3, 3)
                        finish_v(3, 3, g)
                    else:
                        blk = tile_idx - 8

                        def qev(hp, bnk):
                            P.op("dve", lambda eng, hp=hp, bnk=bnk:
                                 eng.tensor_tensor(
                                     out=qs_sb[:, hp * 1024 + blk * 512:
                                               hp * 1024 + (blk + 1) * 512],
                                     in0=pb[bnk][:, 0:512], in1=rb_bc[:],
                                     op=ALU.mult),
                                 reads=[PB(bnk), krb],
                                 writes=[("Qs", blk, hp)])
                        chunks_k(0, 0, wq_sb, "wq")
                        chunks_k(1, 1, wq_sb, "wq")
                        ln_rows_finish(ri=ri, bx=bx, bq=bq, want_bf=True,
                                       want_col=False)
                        finish_k(0, 0, ncq, "ncq", qev)
                        finish_k(1, 1, ncq, "ncq", qev)
                        chunks_k(2, 0, wq_sb, "wq")
                        finish_k(2, 0, ncq, "ncq", qev)
                        chunks_k(3, 1, wq_sb, "wq")
                        finish_k(3, 1, ncq, "ncq", qev)

                for g in range(8):
                    ln_fold_tile(xkv_d[g], g, "kv")
                    wq_head_chain(g)
                colsum(wq_sb, ncq, "ncq", "wq")
                xq_r3 = xq_d.rearrange("p (c b t) -> p c b t", c=NCH, b=2)
                for blk in range(2):
                    ln_fold_tile(xq_r3[:, :, blk, :], 8 + blk, "q")
            P.fence(skip=("pe",))
        P.fence(skip=("pe",))
        if DEBUG:
            P.op("sync", lambda eng: eng.dma_start(out=dbg_wq, in_=wq_sb[:]),
                 reads=[("wq", nn) for nn in range(NKV)],
                 writes=["dbgwq"], dma="qs")
            P.op("sync", lambda eng: eng.dma_start(out=dbg_kc, in_=kc_sb[:]),
                 reads=[("Kc", g, hp) for g in range(8) for hp in range(4)],
                 writes=["dbgkc"], dma="qs")
            P.op("sync", lambda eng: eng.dma_start(out=dbg_vc, in_=vc_sb[:]),
                 reads=[("Vc", sl) for sl in range(32)] + ["vc_ones"],
                 writes=["dbgvc"], dma="qs")
            P.op("sync", lambda eng: eng.dma_start(out=dbg_qs, in_=qs_sb[:]),
                 reads=[("Qs", b, hp) for b in range(2) for hp in range(4)],
                 writes=["dbgqs"], dma="qs")

        xres_scope = ExitStack()
        xres = xres_scope.enter_context(
            nc.sbuf_tensor("xres", [128, NCH * 1024], F32))
        xres_v = xres.ap().rearrange("p (c t) -> p c t", c=NCH)
        P.op("sync", lambda eng: eng.dma_start(out=xres[:], in_=xq_d),
             writes=["xres"], dma="qs")

        def slot_sched(blk):
            out = []
            if blk == 0:
                for s in range(12):
                    out.append((s, "data_a", s))
                for i in range(4):
                    out.append((28 + i, "tri", i))
            else:
                for s in range(24):
                    out.append((s, "data_b", s))
                for i in range(4):
                    out.append((24 + i, "tri", i))
                for s in range(28, 32):
                    out.append((s, "keep", 0))
            return out

        with ExitStack() as p2z:
            zsb = p2z.enter_context(nc.sbuf_tensor("zsb", [128, 2048], BF16))
            expT = [p2z.enter_context(
                nc.sbuf_tensor(f"expT{i}", [128, 1024], BF16))
                for i in range(2)]
            ztmp = p2z.enter_context(nc.sbuf_tensor("ztmp", [64, 512], BF16))
            rbl = p2z.enter_context(nc.sbuf_tensor("rbl", [64, 1024], BF16))
            lbuf = p2z.enter_context(nc.sbuf_tensor("lbuf", [65, 1024], F32))
            sqr = [p2z.enter_context(
                nc.sbuf_tensor(f"sqr{i}", [128, 512], BF16)) for i in range(2)]
            mb_bf = p2z.enter_context(nc.sbuf_tensor("mb_bf", [128, 512], BF16))
            rb_bf = p2z.enter_context(nc.sbuf_tensor("rb_bf", [128, 512], BF16))
            tf = p2z.enter_context(nc.sbuf_tensor("tf", [128, 512], F32))
            wob = [p2z.enter_context(
                nc.sbuf_tensor(f"wob{i}", [128, 1024], BF16)) for i in range(2)]

            def emit_attention(blk):
                sched = slot_sched(blk)
                nslots = len(sched)
                for hp in range(4):
                    def emit_sc(si):
                        s = sched[si][0]
                        g = s // 4
                        sb0 = (si % 2) * 2
                        for par in range(2):
                            lo, hi = 64 * par, 64 * par + 64
                            P.op("pe", lambda eng, hp=hp, s=s, blk=blk,
                                 lo=lo, hi=hi, sb=sb0 + par: eng.matmul(
                                     out=pb[sb][:, 0:512],
                                     lhsT=kc_sb[lo:hi, hp * S + s * 128:
                                                hp * S + (s + 1) * 128],
                                     rhs=qs_sb[lo:hi,
                                               hp * 1024 + blk * 512:
                                               hp * 1024 + (blk + 1) * 512],
                                     start=True, stop=True),
                                 reads=[("Kc", g, hp), ("Qs", blk, hp)],
                                 writes=[PB(sb0 + par)])

                    def emit_tail(si):
                        s, bk, bi = sched[si]
                        sp = si % 2
                        sb0 = sp * 2
                        if bk == "data_a":
                            bias = mba_sb[:, bi:bi + 1]
                            brd = ["mba"]
                        elif bk == "data_b":
                            bias = mbb_sb[:, bi:bi + 1]
                            brd = ["mbb"]
                        else:
                            bias = cst[:, 0:1]
                            brd = ["cst"]
                        P.op("act", lambda eng, sp=sp, bias=bias:
                             eng.activation(
                                 out=expT[sp][:], in_=pbw[sp][:],
                                 func=AF.Exp, bias=bias,
                                 scale=cst[:, 1:2]),
                             reads=[PB(2 * sp), PB(2 * sp + 1), "cst"] + brd,
                             writes=[("expT", sp)])
                        if bk == "tri":
                            c0 = 384 - bi * 128
                            for par in range(2):
                                P.op("dve", lambda eng, sp=sp, par=par,
                                     c0=c0: eng.tensor_tensor(
                                         out=expT[sp][:, par * 512:
                                                      (par + 1) * 512],
                                         in0=expT[sp][:, par * 512:
                                                      (par + 1) * 512],
                                         in1=tri[:, c0:c0 + 512],
                                         op=ALU.mult),
                                     reads=[("expT", sp), "tri"],
                                     writes=[("expT", sp)])
                        for par in range(2):
                            s_ = sched[si][0]
                            n = 2 * hp + par
                            P.op("pe", lambda eng, s=s_, n=n, par=par,
                                 sp=sp, si=si: eng.matmul(
                                     out=pb[4 + par][0:65, 0:512],
                                     lhsT=vc_sb[:, s * 520 + n * 65:
                                                s * 520 + (n + 1) * 65],
                                     rhs=expT[sp][:, par * 512:
                                                  (par + 1) * 512],
                                     start=(si == 0),
                                     stop=(si == nslots - 1),
                                     skip_group_check=True),
                                 reads=[("Vc", s), ("expT", sp), "vc_ones"],
                                 writes=[PB(4 + par)])

                    for si in range(nslots + 1):
                        if si < nslots:
                            emit_sc(si)
                        if si >= 1:
                            emit_tail(si - 1)
                    for par in range(2):
                        P.op("dve", lambda eng, par=par: eng.reciprocal(
                            out=lbuf[64:65, par * 512:(par + 1) * 512],
                            in_=pb[4 + par][64:65, 0:512]),
                            reads=[PB(4 + par)], writes=["lbuf"])
                        P.op("pe", lambda eng, par=par: eng.matmul(
                            out=pb[par][0:64, 0:512],
                            lhsT=ones65r[64:65, 0:64],
                            rhs=lbuf[64:65, par * 512:(par + 1) * 512],
                            start=True, stop=True),
                            reads=["lbuf", "ones"], writes=[PB(par)])
                        P.op("dve", lambda eng, par=par: eng.tensor_copy(
                            out=rbl[:, par * 512:(par + 1) * 512],
                            in_=pb[par][0:64, 0:512]),
                            reads=[PB(par)], writes=["rbl"])
                    P.op("dve", lambda eng, hp=hp: eng.tensor_tensor(
                        out=zsb[0:64, hp * 512:(hp + 1) * 512],
                        in0=pb[4][0:64, 0:512], in1=rbl[:, 0:512],
                        op=ALU.mult),
                        reads=[PB(4), "rbl"], writes=[("zsb", hp, 0)])
                    P.op("dve", lambda eng: eng.tensor_tensor(
                        out=ztmp[:], in0=pb[5][0:64, 0:512],
                        in1=rbl[:, 512:1024], op=ALU.mult),
                        reads=[PB(5), "rbl"], writes=["ztmp"])
                    P.op("sync", lambda eng, hp=hp: eng.dma_start(
                        out=zsb[64:128, hp * 512:(hp + 1) * 512],
                        in_=ztmp[:]),
                        reads=["ztmp"], writes=[("zsb", hp, 1)], dma="qs")

            def emit_oproj(blk, ao):
                zreads = [("zsb", hp, pp) for hp in range(4) for pp in range(2)]
                for m in range(NCH):
                    bnk = 2 + m % 2
                    mp2, mh = divmod(m, 2)
                    wb = wob[mp2 % 2]
                    if mh == 0:
                        P.op("pool", lambda eng, m=m, wb=wb: eng.dma_start(
                            out=wb.ap().rearrange("p (q d) -> p q d", q=2),
                            in_=wo_d[m:m + 2].rearrange("q p d -> p q d")),
                            writes=[("wob", mp2 % 2)], dma="qp")

                    def omm(eng, m=m, bnk=bnk, wb=wb, mh=mh):
                        r = None
                        for kcb in range(4):
                            r = eng.matmul(
                                out=pb[bnk][:, 0:512],
                                lhsT=wb[:, mh * 512 + kcb * 128:
                                        mh * 512 + (kcb + 1) * 128],
                                rhs=zsb[:, kcb * 512:(kcb + 1) * 512],
                                start=(kcb == 0), stop=(kcb == 3))
                        return r
                    P.op("pe", omm, reads=zreads + [("wob", mp2 % 2)],
                         writes=[PB(bnk)])
                    P.op("act", lambda eng, m=m, bnk=bnk, ao=ao: eng.activation(
                        out=ao[:, m * 512:(m + 1) * 512],
                        in_=pb[bnk][:, 0:512], func=AF.Copy),
                        reads=[PB(bnk)], writes=[("ao", m)])
                if DEBUG:
                    P.op("sync", lambda eng, blk=blk: eng.dma_start(
                        out=dbg_zsb[blk], in_=zsb[:]),
                        reads=zreads, writes=["dbgz"], dma="qs")
                    P.op("sync", lambda eng, blk=blk, ao=ao: eng.dma_start(
                        out=dbg_ao[blk], in_=ao[:]),
                        reads=[("ao", m) for m in range(NCH)],
                        writes=["dbgao"], dma="qs")

            def emit_postln(blk, ao):
                ln_stats_rows([(ao[:, c * 512:(c + 1) * 512], ("ao", c))
                               for c in range(NCH)],
                              [(sqr[0][:], ("sqr", 0)),
                               (sqr[1][:], ("sqr", 1))],
                              ri=blk, bx=6, bq=7)
                ln_rows_finish(ri=blk, bx=6, bq=7,
                               mb_bf=(mb_bf[:], "mb_bf"),
                               rb_bf=(rb_bf[:], "rb_bf"))
                for c in range(NCH):
                    P.op("pool", lambda eng, c=c, ao=ao: eng.tensor_tensor(
                        out=sqr[0][:], in0=ao[:, c * 512:(c + 1) * 512],
                        in1=mb_bf[:], op=ALU.subtract),
                        reads=[("ao", c), "mb_bf"], writes=[("sqr", 0)])
                    P.op("dve", lambda eng: eng.tensor_tensor(
                        out=tf[:], in0=sqr[0][:], in1=rb_bf[:],
                        op=ALU.mult),
                        reads=[("sqr", 0), "rb_bf"], writes=["tf"])
                    P.op("dve", lambda eng, c=c, blk=blk:
                         eng.tensor_tensor(
                             out=xres_v[:, c, blk * 512:(blk + 1) * 512],
                             in0=xres_v[:, c, blk * 512:(blk + 1) * 512],
                             in1=tf[:], op=ALU.add),
                         reads=["tf", "xres"], writes=["xres"])

            emit_attention(0)
            pl0 = ExitStack()
            ao0 = pl0.enter_context(
                nc.sbuf_tensor("ao_0", [128, NCH * 512], BF16))
            emit_oproj(0, ao0)
            if kernel_overlap_p2:
                emit_attention(1)
                emit_postln(0, ao0)
            else:
                P.fence()
                emit_postln(0, ao0)
                P.fence()
                emit_attention(1)
                P.fence()
            pl0.close()
            P.fence(skip=("pe",))
            with ExitStack() as pl1:
                ao1 = pl1.enter_context(
                    nc.sbuf_tensor("ao_1", [128, NCH * 512], BF16))
                emit_oproj(1, ao1)
                emit_postln(1, ao1)
        P.fence(skip=("pe",))
        if DEBUG:
            P.op("sync", lambda eng: eng.dma_start(out=dbg_xattn, in_=xres[:]),
                 reads=["xres"], writes=["dbgxa"], dma="qs")
        attn_scope.close()

        out_r = out_d.rearrange("p (c t) -> p c t", c=NCH)
        with ExitStack() as p3:
            S3 = lambda *a, **k: p3.enter_context(nc.sbuf_tensor(*a, **k))
            fuse = S3("fuse", [128, 64 * 512], BF16)
            dn = S3("dn", [128, NCH * 512], BF16)
            hln = S3("hln", [128, NCH * 512], BF16)
            wgb = [S3(f"wgb{i}", [128, 1024], BF16) for i in range(2)]
            wub = [S3(f"wub{i}", [128, 1024], BF16) for i in range(2)]
            wdb = [S3(f"wdb{i}", [128, 1024], BF16) for i in range(2)]
            sil = [S3(f"sil{i}", [128, 512], BF16) for i in range(2)]
            mb_bf3 = S3("mb_bf3", [128, 512], BF16)
            rb_bf3 = S3("rb_bf3", [128, 512], BF16)
            sqr = [S3(f"fsq{i}", [128, 512], BF16) for i in range(2)]
            tf2 = S3("tf2", [128, 512], F32)
            wd_q = wd_d.rearrange("a b p (c d) -> a b p c d", c=32)
            for blk in range(2):
                for c in range(NCH):
                    P.op("dve", lambda eng, c=c, blk=blk: eng.tensor_copy(
                        out=sil[c % 2][:],
                        in_=xres_v[:, c, blk * 512:(blk + 1) * 512]),
                        reads=["xres"], writes=[("sil", c % 2)])
                    P.op("pe", lambda eng, c=c: eng.matmul(
                        out=pb[6][0:1, 0:512], lhsT=ones_c_bf[:],
                        rhs=sil[c % 2][:], start=(c == 0), stop=(c == NCH - 1)),
                        reads=[("sil", c % 2), "ones"], writes=[PB(6)])
                    P.op("dve", lambda eng, c=c: eng.tensor_tensor(
                        out=sqr[c % 2][:], in0=sil[c % 2][:],
                        in1=sil[c % 2][:], op=ALU.mult),
                        reads=[("sil", c % 2)], writes=[("fsq", c % 2)])
                    P.op("pe", lambda eng, c=c: eng.matmul(
                        out=pb[7][0:1, 0:512], lhsT=ones_c_bf[:],
                        rhs=sqr[c % 2][:], start=(c == 0), stop=(c == NCH - 1)),
                        reads=[("fsq", c % 2), "ones"], writes=[PB(7)])
                ln_rows_finish(ri=0, bx=6, bq=7, want_mb=True)
                if DEBUG:
                    pass
                for c in range(NCH):
                    P.op("pool", lambda eng, c=c, blk=blk:
                         eng.tensor_tensor(
                             out=hln[:, c * 512:(c + 1) * 512],
                             in0=xres_v[:, c, blk * 512:(blk + 1) * 512],
                             in1=ROWS[0]["mb_bc"][:], op=ALU.subtract),
                         reads=["xres", ("mb_bc", 0)], writes=[("hln", c)])
                    P.op("dve", lambda eng, c=c: eng.tensor_tensor(
                        out=hln[:, c * 512:(c + 1) * 512],
                        in0=hln[:, c * 512:(c + 1) * 512],
                        in1=ROWS[0]["rb_bc"][:], op=ALU.mult),
                        reads=[("hln", c), ("rb_bc", 0)], writes=[("hln", c)])
                hreads = [("hln", c) for c in range(NCH)]
                if DEBUG:
                    P.op("sync", lambda eng, blk=blk: eng.dma_start(
                        out=dbg_hln[blk], in_=hln[:]),
                        reads=hreads, writes=["dbghln"], dma="qs")
                for ic in range(64):
                    for h in range(2):
                        bi = (ic * 2 + h) % 2
                        P.op("pool", lambda eng, bi=bi, ic=ic, h=h:
                             eng.dma_start(
                                 out=wgb[bi][:], in_=wg_d[ic, h]),
                             writes=[("wgb", bi)], dma="qp")
                        P.op("pool", lambda eng, bi=bi, ic=ic, h=h:
                             eng.dma_start(
                                 out=wub[bi][:], in_=wu_d[ic, h]),
                             writes=[("wub", bi)], dma="qp")

                        def gmm(eng, bi=bi, h=h):
                            r = None
                            for c in range(8):
                                r = eng.matmul(
                                    out=pb[0][:, 0:512],
                                    lhsT=wgb[bi][:, c * 128:(c + 1) * 128],
                                    rhs=hln[:, (h * 8 + c) * 512:
                                             (h * 8 + c + 1) * 512],
                                    start=(h == 0 and c == 0),
                                    stop=(h == 1 and c == 7),
                                    skip_group_check=True)
                            return r
                        P.op("pe", gmm, reads=hreads + [("wgb", bi)],
                             writes=[PB(0)])

                        def umm(eng, bi=bi, h=h):
                            r = None
                            for c in range(8):
                                r = eng.matmul(
                                    out=pb[1][:, 0:512],
                                    lhsT=wub[bi][:, c * 128:(c + 1) * 128],
                                    rhs=hln[:, (h * 8 + c) * 512:
                                             (h * 8 + c + 1) * 512],
                                    start=(h == 0 and c == 0),
                                    stop=(h == 1 and c == 7),
                                    skip_group_check=True)
                            return r
                        P.op("pe", umm, reads=hreads + [("wub", bi)],
                             writes=[PB(1)])
                    P.op("act", lambda eng, ic=ic: eng.activation(
                        out=sil[ic % 2][:], in_=pb[0][:, 0:512],
                        func=AF.Silu, scale=sg_sb[:, ic:ic + 1]),
                        reads=[PB(0), ("sg", ic)],
                        writes=[("sil", ic % 2)])
                    P.op("dve", lambda eng, ic=ic:
                         eng.scalar_tensor_tensor(
                             out=fuse[:, ic * 512:(ic + 1) * 512],
                             in0=pb[1][:, 0:512],
                             scalar=su_sb[:, ic:ic + 1],
                             in1=sil[ic % 2][:], op0=ALU.mult,
                             op1=ALU.mult),
                         reads=[PB(1), ("sil", ic % 2), ("su", ic)],
                         writes=[("fuse", ic)])
                freads = [("fuse", ic) for ic in range(64)]
                for mp in range(8):
                    for q16 in range(16):
                        bi = q16 % 2
                        cih, half = divmod(q16, 8)
                        P.op("pool", lambda eng, bi=bi, mp=mp, cih=cih,
                             half=half: eng.dma_start(
                                 out=wdb[bi][:],
                                 in_=wd_q[mp, cih, :,
                                          half * 4:(half + 1) * 4, :]),
                             writes=[("wdb", bi)], dma="qp")
                        for dm in range(2):
                            m = mp * 2 + dm

                            def dmm(eng, bi=bi, q16=q16, dm=dm):
                                r = None
                                for cc in range(4):
                                    r = eng.matmul(
                                        out=pb[2 + dm][:, 0:512],
                                        lhsT=wdb[bi][:,
                                                     cc * 256 + dm * 128:
                                                     cc * 256 + (dm + 1) * 128],
                                        rhs=fuse[:,
                                                 (q16 * 4 + cc) * 512:
                                                 (q16 * 4 + cc + 1) * 512],
                                        start=(q16 == 0 and cc == 0),
                                        stop=(q16 == 15 and cc == 3),
                                        skip_group_check=True)
                                return r
                            P.op("pe", dmm,
                                 reads=freads + [("wdb", bi)],
                                 writes=[PB(2 + dm)])
                            if q16 == 15:
                                P.op("act", lambda eng, m=m, dm=dm:
                                     eng.activation(
                                         out=dn[:, m * 512:(m + 1) * 512],
                                         in_=pb[2 + dm][:, 0:512],
                                         func=AF.Copy,
                                         scale=sd_sb[:, m:m + 1]),
                                     reads=[PB(2 + dm), ("sd", m)],
                                     writes=[("dn", m)])
                if DEBUG:
                    P.op("sync", lambda eng, blk=blk: eng.dma_start(
                        out=dbg_dn[blk], in_=dn[:]),
                        reads=[("dn", c) for c in range(NCH)],
                        writes=["dbgdn"], dma="qs")
                ln_stats_rows([(dn[:, c * 512:(c + 1) * 512], ("dn", c))
                               for c in range(NCH)],
                              [(sqr[0][:], ("fsq", 0)),
                               (sqr[1][:], ("fsq", 1))],
                              ri=1, bx=4, bq=5)
                ln_rows_finish(ri=1, bx=4, bq=5,
                               mb_bf=(mb_bf3[:], "mb_bf3"),
                               rb_bf=(rb_bf3[:], "rb_bf3"))
                for c in range(NCH):
                    P.op("pool", lambda eng, c=c: eng.tensor_tensor(
                        out=sqr[0][:], in0=dn[:, c * 512:(c + 1) * 512],
                        in1=mb_bf3[:], op=ALU.subtract),
                        reads=[("dn", c), "mb_bf3"], writes=[("fsq", 0)])
                    P.op("dve", lambda eng: eng.tensor_tensor(
                        out=tf2[:], in0=sqr[0][:], in1=rb_bf3[:],
                        op=ALU.mult),
                        reads=[("fsq", 0), "rb_bf3"], writes=["tf2"])
                    P.op("dve", lambda eng, c=c, blk=blk:
                         eng.tensor_tensor(
                             out=xres_v[:, c, blk * 512:(blk + 1) * 512],
                             in0=xres_v[:, c, blk * 512:(blk + 1) * 512],
                             in1=tf2[:], op=ALU.add),
                         reads=["tf2", "xres"],
                         writes=["xres", ("xout", blk, c)])
                P.op("sync", lambda eng, blk=blk: eng.dma_start(
                    out=out_r[:, :, blk * 512:(blk + 1) * 512],
                    in_=xres_v[:, :, blk * 512:(blk + 1) * 512]),
                    reads=[("xout", blk, c) for c in range(NCH)],
                    writes=[("outd", blk)], dma="qs")
        xres_scope.close()

        P.tail_wait("sync", ["qs", "qp"])

        with nc.Block() as block:
            def run_stream(name):
                def _(eng):
                    for waits, emit, inc in P.ops[name]:
                        for (s, v) in waits:
                            eng.wait_ge(sem[s], v)
                        if emit is not None:
                            inst = emit(eng)
                            if inst is None:
                                raise RuntimeError(f"op on {name} returned None")
                            inst.then_inc(sem[inc[0]], inc[1])
                return _

            block.tensor(run_stream("pe"))
            block.vector(run_stream("dve"))
            block.scalar(run_stream("act"))
            block.gpsimd(run_stream("pool"))
            block.sync(run_stream("sync"))

    return nc


_NC_CACHE = None


def _get_nc():
    global _NC_CACHE
    if _NC_CACHE is None:
        _NC_CACHE = build_program()
    return _NC_CACHE


def kernel(L0_attn_normalized_resid_pre, L0_attn_W_Q, L0_attn_W_K, L0_attn_W_V,
           L0_attn_W_O, L0_ffn_w_gate, L0_ffn_w_up, L0_ffn_w_down,
           L0_ffn_s_gate, L0_ffn_s_up, L0_ffn_s_down):
    x = np.asarray(L0_attn_normalized_resid_pre)
    xT = np.ascontiguousarray(x.transpose(0, 2, 1))
    _t = lambda w, h: np.ascontiguousarray(np.asarray(w, dtype=np.float32).reshape(
        h, 16, 128, 64).transpose(0, 2, 1, 3).reshape(h, 128, 1024))
    wq = _t(L0_attn_W_Q, 32)
    wk = _t(L0_attn_W_K, 8)
    wv = _t(L0_attn_W_V, 8)
    wo = np.ascontiguousarray(np.asarray(L0_attn_W_O, dtype=np.float32)
                              .reshape(4, 128, 16, 128)
                              .transpose(2, 1, 0, 3).reshape(16, 128, 512))
    _ffn_tile = lambda w: np.ascontiguousarray(np.asarray(w).reshape(
        2, 8, 128, 64, 128).transpose(3, 0, 2, 1, 4).reshape(64, 2, 128, 1024))
    wg = _ffn_tile(L0_ffn_w_gate)
    wu = _ffn_tile(L0_ffn_w_up)
    wd = np.ascontiguousarray(np.asarray(L0_ffn_w_down).reshape(
        2, 32, 128, 8, 256).transpose(3, 0, 2, 1, 4).reshape(8, 2, 128, 8192))
    sg = np.ascontiguousarray(np.asarray(L0_ffn_s_gate), dtype=np.float32)
    su = np.ascontiguousarray(np.asarray(L0_ffn_s_up), dtype=np.float32)
    sd = np.ascontiguousarray(np.asarray(L0_ffn_s_down), dtype=np.float32)

    in_maps = []
    metas = []
    for core in range(8):
        s = core // 4
        j = core % 4
        A, B = j, 7 - j
        xq0 = np.concatenate(
            [xT[s][:, A * 512:(A + 1) * 512], xT[s][:, B * 512:(B + 1) * 512]],
            axis=1)
        xq = np.ascontiguousarray(
            xq0.reshape(16, 128, 2, 512).transpose(1, 0, 2, 3).reshape(128, 16384))
        others = [b for b in range(8) if b not in (A, B)]
        order = [b for b in others if b < A] + [b for b in others if b >= A]
        cols = [xT[s][:, b * 512:(b + 1) * 512] for b in order]
        cols.append(xT[s][:, B * 512:(B + 1) * 512])
        cols.append(xT[s][:, A * 512:(A + 1) * 512])
        xkv0 = np.concatenate(cols, axis=1)
        xkv = np.ascontiguousarray(
            xkv0.reshape(16, 128, 8, 512).transpose(2, 1, 0, 3).reshape(8, 128, 8192))
        mba = np.full((128, 12), ESHIFT - 10000.0, np.float32)
        mbb = np.full((128, 24), ESHIFT - 10000.0, np.float32)
        for t in range(12):
            if order[t // 4] < A:
                mba[:, t] = ESHIFT
        for t in range(24):
            if order[t // 4] < B:
                mbb[:, t] = ESHIFT
        in_maps.append({
            "xq": xq, "xkv": xkv, "wq": wq, "wk": wk, "wv": wv, "wo": wo,
            "wg": wg, "wu": wu, "wd": wd, "sg": sg, "su": su, "sd": sd,
            "mba": mba, "mbb": mbb,
        })
        metas.append((s, A, B))

    nc = _get_nc()
    res = run_bass_kernel_spmd(nc, in_maps, core_ids=list(range(8)))
    global LAST_RESULTS
    LAST_RESULTS = res if DEBUG else None

    outT = np.empty((2, 2048, 4096), np.float32)
    for core, (s, A, B) in enumerate(metas):
        o = np.asarray(res.results[core]["out"]).reshape(128, 16, 2, 512)
        outT[s][:, A * 512:(A + 1) * 512] = \
            o[:, :, 0, :].transpose(1, 0, 2).reshape(2048, 512)
        outT[s][:, B * 512:(B + 1) * 512] = \
            o[:, :, 1, :].transpose(1, 0, 2).reshape(2048, 512)
    return np.ascontiguousarray(outT.transpose(0, 2, 1))


# version 18 (final)
# speedup vs baseline: 1.1953x; 1.1606x over previous
import numpy as np
from contextlib import ExitStack

import concourse.bass as bass
import concourse.mybir as mybir
from concourse.bass_utils import run_bass_kernel_spmd

F32 = mybir.dt.float32
BF16 = mybir.dt.bfloat16
I8 = mybir.dt.int8
AF = mybir.ActivationFunctionType
ALU = mybir.AluOpType

D = 2048
S = 4096
NKV = 8
INTER = 8192
NCH = D // 128
EPS = 1e-5
ESHIFT = -30.0

ENGINES = ("pe", "dve", "act", "pool", "sync")


class Prog:

    def __init__(self):
        self.ops = {e: [] for e in ENGINES}
        self.cnt = {"pe": 0, "dve": 0, "act": 0, "pool": 0, "qs": 0, "qp": 0}
        self.last_w = {}
        self.readers = {}
        self.waited = {e: {} for e in ENGINES}

    def op(self, engine, emit, reads=(), writes=(), dma=None):
        rd, wr = [], list(writes)
        for k in reads:
            if isinstance(k, tuple) and k and k[0] == "pb":
                wr.append(k)
            else:
                rd.append(k)
        need = {}

        def _need(sem, val):
            if val > need.get(sem, 0):
                need[sem] = val

        for k in rd:
            w = self.last_w.get(k)
            if w:
                _need(*w)
        for k in wr:
            w = self.last_w.get(k)
            if w:
                _need(*w)
            for (rsem, rcnt) in self.readers.get(k, ()):
                if dma is None and rsem == engine:
                    continue
                _need(rsem, rcnt)
        waits = []
        for semn, val in need.items():
            if self.waited[engine].get(semn, 0) < val:
                waits.append((semn, val))
                self.waited[engine][semn] = val
        if dma:
            self.cnt[dma] += 16
            comp = (dma, self.cnt[dma])
            inc = (dma, 16)
        else:
            self.cnt[engine] += 1
            comp = (engine, self.cnt[engine])
            inc = (engine, 1)
        for k in wr:
            self.last_w[k] = comp
            self.readers[k] = []
        for k in rd:
            self.readers.setdefault(k, []).append(comp)
        self.ops[engine].append((waits, emit, inc))

    def fence(self):
        for e in ENGINES:
            waits = []
            for semn, val in self.cnt.items():
                if val > 0 and self.waited[e].get(semn, 0) < val:
                    waits.append((semn, val))
                    self.waited[e][semn] = val
            if waits:
                self.ops[e].append((waits, None, None))

    def tail_wait(self, engine, sems):
        waits = []
        for semn in sems:
            val = self.cnt[semn]
            if val > 0 and self.waited[engine].get(semn, 0) < val:
                waits.append((semn, val))
                self.waited[engine][semn] = val
        if waits:
            self.ops[engine].append((waits, None, None))


DEBUG = False
LAST_RESULTS = None


def build_program():
    nc = bass.Bass("TRN2", target_bir_lowering=False)
    P = Prog()

    xq_d = nc.dram_tensor("xq", [128, NCH * 1024], F32, kind="ExternalInput").ap()
    xkv_d = nc.dram_tensor("xkv", [8, 128, NCH * 512], F32, kind="ExternalInput").ap()
    wq_d = nc.dram_tensor("wq", [32, 128, 1024], F32, kind="ExternalInput").ap()
    wk_d = nc.dram_tensor("wk", [NKV, 128, 1024], F32, kind="ExternalInput").ap()
    wv_d = nc.dram_tensor("wv", [NKV, 128, 1024], F32, kind="ExternalInput").ap()
    wo_d = nc.dram_tensor("wo", [512, D], F32, kind="ExternalInput").ap()
    wg_d = nc.dram_tensor("wg", [64, 2, 128, 1024], I8, kind="ExternalInput").ap()
    wu_d = nc.dram_tensor("wu", [64, 2, 128, 1024], I8, kind="ExternalInput").ap()
    wd_d = nc.dram_tensor("wd", [8, 2, 128, 32 * 256], I8, kind="ExternalInput").ap()
    sg_d = nc.dram_tensor("sg", [INTER], F32, kind="ExternalInput").ap()
    su_d = nc.dram_tensor("su", [INTER], F32, kind="ExternalInput").ap()
    sd_d = nc.dram_tensor("sd", [D], F32, kind="ExternalInput").ap()
    mba_d = nc.dram_tensor("mba", [128, 12], F32, kind="ExternalInput").ap()
    mbb_d = nc.dram_tensor("mbb", [128, 24], F32, kind="ExternalInput").ap()
    out_d = nc.dram_tensor("out", [128, NCH * 1024], F32, kind="ExternalOutput").ap()
    if DEBUG:
        dbg_kc = nc.dram_tensor("dbg_kc", [128, 4 * S], BF16,
                                kind="ExternalOutput").ap()
        dbg_vc = nc.dram_tensor("dbg_vc", [128, 32 * 520], BF16,
                                kind="ExternalOutput").ap()
        dbg_qs = nc.dram_tensor("dbg_qs", [128, 4 * 1024], BF16,
                                kind="ExternalOutput").ap()
        dbg_zsb = nc.dram_tensor("dbg_zsb", [2, 128, 2048], BF16,
                                 kind="ExternalOutput").ap()
        dbg_ao = nc.dram_tensor("dbg_ao", [2, 128, NCH * 512], BF16,
                                kind="ExternalOutput").ap()
        dbg_xattn = nc.dram_tensor("dbg_xattn", [128, NCH * 1024], F32,
                                   kind="ExternalOutput").ap()
        dbg_wq = nc.dram_tensor("dbg_wq", [128, NCH * 512], BF16,
                                kind="ExternalOutput").ap()
        dbg_hln = nc.dram_tensor("dbg_hln", [2, 128, NCH * 512], BF16,
                                 kind="ExternalOutput").ap()
        dbg_dn = nc.dram_tensor("dbg_dn", [2, 128, NCH * 512], BF16,
                                kind="ExternalOutput").ap()

    with ExitStack() as top:
        SB = lambda *a, **k: top.enter_context(nc.sbuf_tensor(*a, **k))
        pbw = [top.enter_context(nc.psum_tensor(f"pbw{i}", [128, 1024], F32))
               for i in range(2)]
        pbs = [top.enter_context(nc.psum_tensor(f"pb{i}", [128, 512], F32))
               for i in range(4, 8)]
        pb = {}
        for i in range(4):
            pb[i] = pbw[i // 2].ap()[:, (i % 2) * 1024 // 2:(i % 2) * 512 + 512]
        for i in range(4):
            pb[4 + i] = pbs[i].ap()
        PB = lambda i: ("pb", i)
        sem = {name: top.enter_context(nc.semaphore(f"sem_{name}"))
               for name in ("pe", "dve", "act", "pool", "qs", "qp")}

        tri = SB("tri", [128, 896], BF16)
        iden = SB("iden", [128, 128], F32)
        iden_bf = SB("iden_bf", [128, 128], BF16)
        ones_c_bf = SB("ones_c_bf", [128, 1], BF16)
        ones_r_f = SB("ones_r_f", [1, 128], F32)
        ones65r = SB("ones65r", [65, 64], F32)
        sg_sb = SB("sg_sb", [128, 64], F32)
        su_sb = SB("su_sb", [128, 64], F32)
        sd_sb = SB("sd_sb", [128, 16], F32)
        mba_sb = SB("mba_sb", [128, 12], F32)
        mbb_sb = SB("mbb_sb", [128, 24], F32)
        ncq = SB("ncq", [1, 512], BF16)
        nck = SB("nck", [1, 512], BF16)
        ncv = SB("ncv", [1, 512], BF16)
        mrow_f = SB("mrow_f", [1, 512], F32)
        qrow_f = SB("qrow_f", [1, 512], F32)
        m2row = SB("m2row", [1, 512], F32)
        rrow_f = SB("rrow_f", [1, 512], F32)
        mrow_bf = SB("mrow_bf", [1, 512], BF16)
        rcol = SB("rcol", [128, 4], F32)
        rb_bc = SB("rb_bc", [128, 512], F32)
        mb_bc = SB("mb_bc", [128, 512], F32)
        cst = SB("cst", [128, 8], F32)
        CST_ESHIFT, CST_SC8, CST_INVD, CST_EPS, CST_NEG1 = 0, 1, 2, 3, 4

        attn_scope = ExitStack()
        SR = lambda *a, **k: attn_scope.enter_context(
            nc.sbuf_tensor(*a, side="right", **k))
        wo_sb = SR("wo_sb", [128, 4 * D], BF16)
        kc_sb = SR("kc_sb", [128, 4 * S], BF16)
        vc_sb = SR("vc_sb", [128, 32 * 520], BF16)
        qs_sb = SR("qs_sb", [128, 4 * 1024], BF16)

        def E(fn):
            return fn

        def ln_stats_rows(xsrc, sqbufs):
            n = len(xsrc)
            for c in range(n):
                ap_c, k_c = xsrc[c]
                sq_ap, sq_key = sqbufs[c % 2]
                P.op("pe", lambda eng, ap=ap_c, c=c, n=n: eng.matmul(
                    out=pb[6][0:1, 0:512], lhsT=ones_c_bf[:], rhs=ap,
                    start=(c == 0), stop=(c == n - 1)),
                    reads=[k_c, "ones"], writes=[PB(6)])
                P.op("dve", lambda eng, ap=ap_c, sq=sq_ap: eng.tensor_tensor(
                    out=sq, in0=ap, in1=ap, op=ALU.mult),
                    reads=[k_c], writes=[sq_key])
                P.op("pe", lambda eng, sq=sq_ap, c=c, n=n: eng.matmul(
                    out=pb[7][0:1, 0:512], lhsT=ones_c_bf[:], rhs=sq,
                    start=(c == 0), stop=(c == n - 1)),
                    reads=[sq_key, "ones"], writes=[PB(7)])

        def ln_rows_finish(want_bf=False, want_col=False, want_mb=False,
                           mb_bf=None, rb_bf=None):
            P.op("act", lambda eng: eng.activation(
                out=mrow_f[:], in_=pb[6][0:1, 0:512], func=AF.Copy,
                scale=cst[0:1, 2:3]), reads=[PB(6), "cst"], writes=["mrow_f"])
            P.op("act", lambda eng: eng.activation(
                out=qrow_f[:], in_=pb[7][0:1, 0:512], func=AF.Copy,
                scale=cst[0:1, 2:3]), reads=[PB(7), "cst"], writes=["qrow_f"])
            P.op("dve", lambda eng: eng.tensor_tensor(
                out=m2row[:], in0=mrow_f[:], in1=mrow_f[:], op=ALU.mult),
                reads=["mrow_f"], writes=["m2row"])
            P.op("dve", lambda eng: eng.tensor_tensor(
                out=qrow_f[:], in0=qrow_f[:], in1=m2row[:], op=ALU.subtract),
                reads=["qrow_f", "m2row"], writes=["qrow_f"])
            P.op("act", lambda eng: eng.activation(
                out=m2row[:], in_=qrow_f[:], func=AF.Sqrt,
                bias=cst[0:1, 3:4]),
                reads=["qrow_f", "cst"], writes=["m2row"])
            P.op("dve", lambda eng: eng.reciprocal(out=rrow_f[:], in_=m2row[:]),
                reads=["m2row"], writes=["rrow_f"])
            if want_bf:
                P.op("dve", lambda eng: eng.tensor_copy(
                    out=mrow_bf[:], in_=mrow_f[:]),
                    reads=["mrow_f"], writes=["mrow_bf"])
            P.op("pe", lambda eng: eng.matmul(
                out=pb[6][:, 0:512], lhsT=ones_r_f[:], rhs=rrow_f[:],
                start=True, stop=True), reads=["rrow_f", "ones"],
                writes=[PB(6)])
            P.op("act", lambda eng: eng.activation(
                out=rb_bc[:], in_=pb[6][:, 0:512], func=AF.Copy),
                reads=[PB(6)], writes=["rb_bc"])
            if rb_bf is not None:
                ap, key = rb_bf
                P.op("dve", lambda eng, ap=ap: eng.tensor_copy(
                    out=ap, in_=rb_bc[:]), reads=["rb_bc"], writes=[key])
            if want_mb or mb_bf is not None:
                P.op("pe", lambda eng: eng.matmul(
                    out=pb[6][:, 0:512], lhsT=ones_r_f[:], rhs=mrow_f[:],
                    start=True, stop=True), reads=["mrow_f", "ones"],
                    writes=[PB(6)])
                P.op("act", lambda eng: eng.activation(
                    out=mb_bc[:], in_=pb[6][:, 0:512], func=AF.Copy),
                    reads=[PB(6)], writes=["mb_bc"])
                if mb_bf is not None:
                    ap, key = mb_bf
                    P.op("dve", lambda eng, ap=ap: eng.tensor_copy(
                        out=ap, in_=mb_bc[:]), reads=["mb_bc"], writes=[key])
            if want_col:
                for ts in range(4):
                    P.op("pe", lambda eng, ts=ts: eng.matmul(
                        out=pb[6][0:128, ts:ts + 1],
                        lhsT=rrow_f[0:1, ts * 128:(ts + 1) * 128],
                        rhs=ones_r_f[0:1, 0:1],
                        start=True, stop=True),
                        reads=["rrow_f", "ones"], writes=[PB(6)])
                P.op("dve", lambda eng: eng.tensor_copy(
                    out=rcol[:], in_=pb[6][0:128, 0:4]),
                    reads=[PB(6)], writes=["rcol"])

        P.op("sync", lambda eng: eng.dma_start(out=mba_sb[:], in_=mba_d),
             writes=["mba"], dma="qs")
        P.op("sync", lambda eng: eng.dma_start(out=mbb_sb[:], in_=mbb_d),
             writes=["mbb"], dma="qs")
        for (dst, src, n, nm) in ((sg_sb, sg_d, 64, "sg"), (su_sb, su_d, 64, "su"),
                                  (sd_sb, sd_d, 16, "sd")):
            for c in range(n):
                src_ap = bass.AP(src.tensor, c * 128, [[1, 128], [1, 1]])
                P.op("sync", lambda eng, d=dst, c=c, s=src_ap: eng.dma_start(
                    out=d[:, c:c + 1], in_=s), writes=[(nm, c)], dma="qs")
        with ExitStack() as msk_scope:
            wt = [msk_scope.enter_context(
                nc.sbuf_tensor(f"wt{i}", [128, 1024], BF16)) for i in range(4)]
            tmp_scope = ExitStack()
            itmp = tmp_scope.enter_context(
                nc.sbuf_tensor("itmp", [128, 896], F32))
            P.op("pool", lambda eng: eng.iota(
                itmp[:, 0:896], [[1, 896]], base=0, channel_multiplier=-1,
                allow_small_or_imprecise_dtypes=True), writes=["itmp"])
            P.op("pool", lambda eng: eng.tensor_scalar(
                out=tri[:], in0=itmp[:, 0:896], scalar1=384.0, scalar2=None,
                op0=ALU.is_ge), reads=["itmp"], writes=["tri"])
            P.op("pool", lambda eng: eng.iota(
                itmp[:, 0:128], [[1, 128]], base=0, channel_multiplier=-1,
                allow_small_or_imprecise_dtypes=True), writes=["itmp"])
            P.op("pool", lambda eng: eng.tensor_scalar(
                out=iden[:], in0=itmp[:, 0:128], scalar1=0.0, scalar2=None,
                op0=ALU.is_equal), reads=["itmp"], writes=["iden"])
            P.op("pool", lambda eng: eng.tensor_copy(
                out=iden_bf[:], in_=iden[:]), reads=["iden"],
                writes=["iden_bf"])
            P.op("dve", lambda eng: eng.memset(ones_c_bf[:], 1.0),
                 writes=["ones"])
            P.op("dve", lambda eng: eng.memset(ones_r_f[:], 1.0),
                 writes=["ones"])
            P.op("dve", lambda eng: eng.memset(ones65r[64:65, :], 1.0),
                 writes=["ones"])
            for col, val in ((0, ESHIFT), (1, 0.125), (2, 1.0 / D),
                             (3, EPS), (4, -1.0)):
                P.op("dve", lambda eng, col=col, val=val: eng.memset(
                    cst[:, col:col + 1], val), writes=["cst"])
            P.op("dve", lambda eng: eng.memset(vc_sb[:], 1.0),
                writes=["vc_ones"])

            wq_sb = msk_scope.enter_context(
                nc.sbuf_tensor("wq_sb", [128, NCH * 512], BF16, side="right"))
            wk_sb = msk_scope.enter_context(
                nc.sbuf_tensor("wk_sb", [128, NCH * 512], BF16, side="right"))
            wv_sb = msk_scope.enter_context(
                nc.sbuf_tensor("wv_sb", [128, NCH * 512], BF16, side="right"))
            for n in range(NKV):
                for (wsb, wd_, nm) in ((wk_sb, wk_d, "wkk"), (wv_sb, wv_d, "wvv")):
                    dstv = wsb.ap().rearrange(
                        "p (c m) -> p c m", c=NCH)[:, :, n * 64:(n + 1) * 64]
                    P.op("pool", lambda eng, d=dstv, wd_=wd_, n=n: eng.dma_start(
                        out=d, in_=wd_[n].rearrange("p (c k) -> p c k", c=NCH)),
                        writes=[(nm, n)], dma="qp")
            for kcb in range(4):
                P.op("pool", lambda eng, kcb=kcb: eng.dma_start(
                    out=wo_sb[:, kcb * D:(kcb + 1) * D],
                    in_=wo_d[kcb * 128:(kcb + 1) * 128, :]),
                    writes=[("wo", kcb)], dma="qp")
            def wq_head_chain(n):
                for g in range(4):
                    P.op("pool", lambda eng, n=n, g=g: eng.dma_start(
                        out=wt[g][:], in_=wq_d[4 * n + g]),
                        writes=[("wt", g)], dma="qp")
                for half in range(2):
                    def smm(eng, half=half):
                        r = None
                        for g in range(4):
                            r = eng.matmul(
                                out=pb[6][:, 0:512], lhsT=iden_bf[:],
                                rhs=wt[g][:, half * 512:(half + 1) * 512],
                                start=(g == 0), stop=(g == 3))
                        return r
                    P.op("pe", smm,
                         reads=[("wt", g) for g in range(4)] + ["iden_bf"],
                         writes=[PB(6)])
                    ws, wsk = vstage[half], ("vst", half)
                    P.op("dve", lambda eng, ws=ws: eng.tensor_copy(
                        out=ws[:], in_=pb[6][:, 0:512]),
                        reads=[PB(6)], writes=[wsk])
                    dst = wq_sb.ap().rearrange(
                        "p (c m) -> p c m",
                        c=NCH)[:, half * 8:(half + 1) * 8, n * 64:(n + 1) * 64]
                    P.op("sync", lambda eng, d=dst, ws=ws: eng.dma_start(
                        out=d, in_=ws.ap().rearrange("p (c k) -> p c k", c=8)),
                        reads=[wsk], writes=[("wq", n)], dma="qs")
            def colsum(wsb, dstn, nm, wkey):
                for c in range(NCH):
                    P.op("pe", lambda eng, wsb=wsb, c=c: eng.matmul(
                        out=pb[7][0:1, 0:512], lhsT=ones_c_bf[:],
                        rhs=wsb[:, c * 512:(c + 1) * 512],
                        start=(c == 0), stop=(c == NCH - 1)),
                        reads=[(wkey, nn) for nn in range(NKV)] + ["ones"],
                        writes=[PB(7)])
                P.op("act", lambda eng, dstn=dstn: eng.activation(
                    out=dstn[:], in_=pb[7][0:1, 0:512], func=AF.Copy,
                    scale=cst[0:1, 4:5]), reads=[PB(7), "cst"], writes=[nm])

            colsum(wk_sb, nck, "nck", "wkk")
            colsum(wv_sb, ncv, "ncv", "wvv")

            P.fence()
            tmp_scope.close()
            with ExitStack() as p1:
                S1 = lambda *a, **k: p1.enter_context(nc.sbuf_tensor(*a, **k))
                xb = [S1(f"xb{i}", [128, NCH * 512], BF16) for i in range(2)]
                sqb = [S1(f"sqb{i}", [128, 512], BF16) for i in range(2)]
                vstage = [S1(f"vstage{i}", [128, 512], BF16) for i in range(2)]

                def ln_fold_tile(src_ap, tile_idx, kind):
                    bi = tile_idx % 2
                    xbt, xbk = xb[bi], ("xb", bi)
                    P.op("pool", lambda eng: eng.dma_start(
                        out=xbt[:], in_=src_ap), writes=[xbk], dma="qp")
                    ln_stats_rows([(xbt[:, c * 512:(c + 1) * 512], xbk)
                                   for c in range(NCH)],
                                  [(sqb[0][:], ("sq", 0)),
                                   (sqb[1][:], ("sq", 1))])

                    def chunks(hp, bnk, wsb, wkey):
                        def kmm(eng, hp=hp, bnk=bnk):
                            r = None
                            for c in range(NCH):
                                r = eng.matmul(
                                    out=pb[bnk][:, 0:512],
                                    lhsT=wsb[:, c * 512 + hp * 128:
                                             c * 512 + (hp + 1) * 128],
                                    rhs=xbt[:, c * 512:(c + 1) * 512],
                                    start=(c == 0), stop=False)
                            return r
                        P.op("pe", kmm,
                             reads=[xbk] + [(wkey, nn) for nn in range(NKV)],
                             writes=[PB(bnk)])

                    def chunks_v(ts, bnk):
                        def vmm(eng, ts=ts, bnk=bnk):
                            r = None
                            for c in range(NCH):
                                r = eng.matmul(
                                    out=pb[bnk][:, 0:512],
                                    lhsT=xbt[:, c * 512 + ts * 128:
                                             c * 512 + (ts + 1) * 128],
                                    rhs=wv_sb[:, c * 512:(c + 1) * 512],
                                    start=(c == 0), stop=False)
                            return r
                        P.op("pe", vmm,
                             reads=[xbk] + [("wvv", nn) for nn in range(NKV)],
                             writes=[PB(bnk)])

                    def fin_k(hp, bnk, ncs, nkey, ev):
                        P.op("pe", lambda eng, hp=hp, bnk=bnk: eng.matmul(
                            out=pb[bnk][:, 0:512],
                            lhsT=ncs[0:1, hp * 128:(hp + 1) * 128],
                            rhs=mrow_bf[:], start=False, stop=True,
                            skip_group_check=True),
                            reads=["mrow_bf", nkey], writes=[PB(bnk)])
                        ev(hp, bnk)

                    def fin_v(ts, bnk, g):
                        P.op("pe", lambda eng, ts=ts, bnk=bnk: eng.matmul(
                            out=pb[bnk][:, 0:512],
                            lhsT=mrow_bf[0:1, ts * 128:(ts + 1) * 128],
                            rhs=ncv[:], start=False, stop=True,
                            skip_group_check=True),
                            reads=["mrow_bf", "ncv"], writes=[PB(bnk)])
                        slot = g * 4 + ts
                        vs, vsk = vstage[ts % 2], ("vst", ts % 2)
                        P.op("dve", lambda eng, vs=vs, ts=ts, bnk=bnk:
                             eng.tensor_scalar(
                                 out=vs[:], in0=pb[bnk][:, 0:512],
                                 scalar1=rcol[:, ts:ts + 1], scalar2=None,
                                 op0=ALU.mult),
                             reads=[PB(bnk), "rcol"], writes=[vsk])
                        vdst = vc_sb.ap()[:, slot * 520:(slot + 1) * 520] \
                            .rearrange("p (n h) -> p n h", n=8)[:, :, 0:64]
                        P.op("sync", lambda eng, vdst=vdst, vs=vs:
                             eng.dma_start(
                                 out=vdst,
                                 in_=vs.ap().rearrange("p (n h) -> p n h",
                                                       n=8)),
                             reads=[vsk, "vc_ones"],
                             writes=[("Vc", slot)], dma="qs")

                    if kind == "kv":
                        g = tile_idx

                        def kev(hp, bnk):
                            P.op("dve", lambda eng, hp=hp, bnk=bnk:
                                 eng.tensor_tensor(
                                     out=kc_sb[:, hp * S + g * 512:
                                               hp * S + (g + 1) * 512],
                                     in0=pb[bnk][:, 0:512], in1=rb_bc[:],
                                     op=ALU.mult),
                                 reads=[PB(bnk), "rb_bc"],
                                 writes=[("Kc", g, hp)])
                        chunks(0, 0, wk_sb, "wkk")
                        chunks(1, 1, wk_sb, "wkk")
                        chunks_v(0, 2)
                        chunks_v(1, 3)
                        ln_rows_finish(want_bf=True, want_col=True)
                        fin_k(0, 0, nck, "nck", kev)
                        fin_k(1, 1, nck, "nck", kev)
                        fin_v(0, 2, g)
                        fin_v(1, 3, g)
                        chunks(2, 0, wk_sb, "wkk")
                        fin_k(2, 0, nck, "nck", kev)
                        chunks(3, 1, wk_sb, "wkk")
                        fin_k(3, 1, nck, "nck", kev)
                        chunks_v(2, 2)
                        fin_v(2, 2, g)
                        chunks_v(3, 3)
                        fin_v(3, 3, g)
                    else:
                        blk = tile_idx - 8

                        def qev(hp, bnk):
                            P.op("dve", lambda eng, hp=hp, bnk=bnk:
                                 eng.tensor_tensor(
                                     out=qs_sb[:, hp * 1024 + blk * 512:
                                               hp * 1024 + (blk + 1) * 512],
                                     in0=pb[bnk][:, 0:512], in1=rb_bc[:],
                                     op=ALU.mult),
                                 reads=[PB(bnk), "rb_bc"],
                                 writes=[("Qs", blk, hp)])
                        chunks(0, 0, wq_sb, "wq")
                        chunks(1, 1, wq_sb, "wq")
                        chunks(2, 2, wq_sb, "wq")
                        chunks(3, 3, wq_sb, "wq")
                        ln_rows_finish(want_bf=True, want_col=False)
                        fin_k(0, 0, ncq, "ncq", qev)
                        fin_k(1, 1, ncq, "ncq", qev)
                        fin_k(2, 2, ncq, "ncq", qev)
                        fin_k(3, 3, ncq, "ncq", qev)

                for g in range(8):
                    ln_fold_tile(xkv_d[g], g, "kv")
                    wq_head_chain(g)
                colsum(wq_sb, ncq, "ncq", "wq")
                xq_r3 = xq_d.rearrange("p (c b t) -> p c b t", c=NCH, b=2)
                for blk in range(2):
                    ln_fold_tile(xq_r3[:, :, blk, :], 8 + blk, "q")
            P.fence()
        P.fence()
        if DEBUG:
            P.op("sync", lambda eng: eng.dma_start(out=dbg_wq, in_=wq_sb[:]),
                 reads=[("wq", nn) for nn in range(NKV)],
                 writes=["dbgwq"], dma="qs")
            P.op("sync", lambda eng: eng.dma_start(out=dbg_kc, in_=kc_sb[:]),
                 reads=[("Kc", g, hp) for g in range(8) for hp in range(4)],
                 writes=["dbgkc"], dma="qs")
            P.op("sync", lambda eng: eng.dma_start(out=dbg_vc, in_=vc_sb[:]),
                 reads=[("Vc", sl) for sl in range(32)] + ["vc_ones"],
                 writes=["dbgvc"], dma="qs")
            P.op("sync", lambda eng: eng.dma_start(out=dbg_qs, in_=qs_sb[:]),
                 reads=[("Qs", b, hp) for b in range(2) for hp in range(4)],
                 writes=["dbgqs"], dma="qs")

        xres_scope = ExitStack()
        xres = xres_scope.enter_context(
            nc.sbuf_tensor("xres", [128, NCH * 1024], F32))
        xres_v = xres.ap().rearrange("p (c t) -> p c t", c=NCH)
        P.op("sync", lambda eng: eng.dma_start(out=xres[:], in_=xq_d),
             writes=["xres"], dma="qs")

        def slot_sched(blk):
            out = []
            if blk == 0:
                for s in range(12):
                    out.append((s, "data_a", s))
                for i in range(4):
                    out.append((28 + i, "tri", i))
            else:
                for s in range(24):
                    out.append((s, "data_b", s))
                for i in range(4):
                    out.append((24 + i, "tri", i))
                for s in range(28, 32):
                    out.append((s, "keep", 0))
            return out

        with ExitStack() as p2z:
            zsb = p2z.enter_context(nc.sbuf_tensor("zsb", [128, 2048], BF16))
            ao = p2z.enter_context(nc.sbuf_tensor("ao", [128, NCH * 512], BF16))
            for blk in range(2):
                sched = slot_sched(blk)
                nslots = len(sched)
                with ExitStack() as p2a:
                    expT = [p2a.enter_context(
                        nc.sbuf_tensor(f"expT{i}_{blk}", [128, 1024], BF16))
                        for i in range(2)]
                    ztmp = p2a.enter_context(
                        nc.sbuf_tensor(f"ztmp_{blk}", [64, 512], BF16))
                    rbl = p2a.enter_context(
                        nc.sbuf_tensor(f"rbl_{blk}", [64, 1024], BF16))
                    lbuf = p2a.enter_context(
                        nc.sbuf_tensor(f"lbuf_{blk}", [65, 1024], F32))
                    for hp in range(4):
                        def emit_sc(si):
                            s = sched[si][0]
                            g = s // 4
                            sb0 = (si % 2) * 2
                            for par in range(2):
                                lo, hi = 64 * par, 64 * par + 64
                                P.op("pe", lambda eng, hp=hp, s=s, blk=blk,
                                     lo=lo, hi=hi, sb=sb0 + par: eng.matmul(
                                         out=pb[sb][:, 0:512],
                                         lhsT=kc_sb[lo:hi, hp * S + s * 128:
                                                    hp * S + (s + 1) * 128],
                                         rhs=qs_sb[lo:hi,
                                                   hp * 1024 + blk * 512:
                                                   hp * 1024 + (blk + 1) * 512],
                                         start=True, stop=True),
                                     reads=[("Kc", g, hp), ("Qs", blk, hp)],
                                     writes=[PB(sb0 + par)])

                        def emit_tail(si):
                            s, bk, bi = sched[si]
                            sp = si % 2
                            sb0 = sp * 2
                            if bk == "data_a":
                                bias = mba_sb[:, bi:bi + 1]
                                brd = ["mba"]
                            elif bk == "data_b":
                                bias = mbb_sb[:, bi:bi + 1]
                                brd = ["mbb"]
                            else:
                                bias = cst[:, 0:1]
                                brd = ["cst"]
                            P.op("act", lambda eng, sp=sp, bias=bias:
                                 eng.activation(
                                     out=expT[sp][:], in_=pbw[sp][:],
                                     func=AF.Exp, bias=bias,
                                     scale=cst[:, 1:2]),
                                 reads=[PB(sb0), PB(sb0 + 1), "cst"] + brd,
                                 writes=[("expT", sp)])
                            if bk == "tri":
                                c0 = 384 - bi * 128
                                for par in range(2):
                                    P.op("dve", lambda eng, sp=sp, par=par,
                                         c0=c0: eng.tensor_tensor(
                                             out=expT[sp][:, par * 512:
                                                          (par + 1) * 512],
                                             in0=expT[sp][:, par * 512:
                                                          (par + 1) * 512],
                                             in1=tri[:, c0:c0 + 512],
                                             op=ALU.mult),
                                         reads=[("expT", sp), "tri"],
                                         writes=[("expT", sp)])
                            for par in range(2):
                                s_ = sched[si][0]
                                n = 2 * hp + par
                                P.op("pe", lambda eng, s=s_, n=n, par=par,
                                     sp=sp, si=si: eng.matmul(
                                         out=pb[4 + par][0:65, 0:512],
                                         lhsT=vc_sb[:, s * 520 + n * 65:
                                                    s * 520 + (n + 1) * 65],
                                         rhs=expT[sp][:, par * 512:
                                                      (par + 1) * 512],
                                         start=(si == 0),
                                         stop=(si == nslots - 1),
                                         skip_group_check=True),
                                     reads=[("Vc", s), ("expT", sp), "vc_ones"],
                                     writes=[PB(4 + par)])

                        for si in range(nslots + 1):
                            if si < nslots:
                                emit_sc(si)
                            if si >= 1:
                                emit_tail(si - 1)
                        for par in range(2):
                            P.op("dve", lambda eng, par=par: eng.reciprocal(
                                out=lbuf[64:65, par * 512:(par + 1) * 512],
                                in_=pb[4 + par][64:65, 0:512]),
                                reads=[PB(4 + par)], writes=["lbuf"])
                            P.op("pe", lambda eng, par=par: eng.matmul(
                                out=pb[par][0:64, 0:512],
                                lhsT=ones65r[64:65, 0:64],
                                rhs=lbuf[64:65, par * 512:(par + 1) * 512],
                                start=True, stop=True),
                                reads=["lbuf", "ones"], writes=[PB(par)])
                            P.op("dve", lambda eng, par=par: eng.tensor_copy(
                                out=rbl[:, par * 512:(par + 1) * 512],
                                in_=pb[par][0:64, 0:512]),
                                reads=[PB(par)], writes=["rbl"])
                        P.op("dve", lambda eng, hp=hp: eng.tensor_tensor(
                            out=zsb[0:64, hp * 512:(hp + 1) * 512],
                            in0=pb[4][0:64, 0:512], in1=rbl[:, 0:512],
                            op=ALU.mult),
                            reads=[PB(4), "rbl"], writes=[("zsb", hp, 0)])
                        P.op("dve", lambda eng: eng.tensor_tensor(
                            out=ztmp[:], in0=pb[5][0:64, 0:512],
                            in1=rbl[:, 512:1024], op=ALU.mult),
                            reads=[PB(5), "rbl"], writes=["ztmp"])
                        P.op("sync", lambda eng, hp=hp: eng.dma_start(
                            out=zsb[64:128, hp * 512:(hp + 1) * 512],
                            in_=ztmp[:]),
                            reads=["ztmp"], writes=[("zsb", hp, 1)], dma="qs")
                    zreads = [("zsb", hp, pp) for hp in range(4)
                              for pp in range(2)]
                    for m in range(NCH):
                        bnk = 2 + m % 2

                        def omm(eng, m=m, bnk=bnk):
                            r = None
                            for kcb in range(4):
                                r = eng.matmul(
                                    out=pb[bnk][:, 0:512],
                                    lhsT=wo_sb[:, kcb * D + m * 128:
                                               kcb * D + (m + 1) * 128],
                                    rhs=zsb[:, kcb * 512:(kcb + 1) * 512],
                                    start=(kcb == 0), stop=(kcb == 3))
                            return r
                        P.op("pe", omm,
                             reads=zreads + [("wo", kcb) for kcb in range(4)],
                             writes=[PB(bnk)])
                        P.op("act", lambda eng, m=m, bnk=bnk: eng.activation(
                            out=ao[:, m * 512:(m + 1) * 512],
                            in_=pb[bnk][:, 0:512], func=AF.Copy),
                            reads=[PB(bnk)], writes=[("ao", m)])
                    if DEBUG:
                        P.op("sync", lambda eng, blk=blk: eng.dma_start(
                            out=dbg_zsb[blk], in_=zsb[:]),
                            reads=zreads, writes=["dbgz"], dma="qs")
                        P.op("sync", lambda eng, blk=blk: eng.dma_start(
                            out=dbg_ao[blk], in_=ao[:]),
                            reads=[("ao", m) for m in range(NCH)],
                            writes=["dbgao"], dma="qs")
                P.fence()
                with ExitStack() as pl:
                    sqr = [pl.enter_context(
                        nc.sbuf_tensor(f"sqr{i}_{blk}", [128, 512], BF16))
                        for i in range(2)]
                    mb_bf = pl.enter_context(
                        nc.sbuf_tensor(f"mb_bf_{blk}", [128, 512], BF16))
                    rb_bf = pl.enter_context(
                        nc.sbuf_tensor(f"rb_bf_{blk}", [128, 512], BF16))
                    tb = pl.enter_context(
                        nc.sbuf_tensor(f"tb_{blk}", [128, 512], BF16))
                    tf = pl.enter_context(
                        nc.sbuf_tensor(f"tf_{blk}", [128, 512], F32))
                    ln_stats_rows([(ao[:, c * 512:(c + 1) * 512], ("ao", c))
                                   for c in range(NCH)],
                                  [(sqr[0][:], ("sqr", 0)),
                                   (sqr[1][:], ("sqr", 1))])
                    ln_rows_finish(mb_bf=(mb_bf[:], "mb_bf"),
                                   rb_bf=(rb_bf[:], "rb_bf"))
                    for c in range(NCH):
                        P.op("pool", lambda eng, c=c: eng.tensor_tensor(
                            out=tb[:], in0=ao[:, c * 512:(c + 1) * 512],
                            in1=mb_bf[:], op=ALU.subtract),
                            reads=[("ao", c), "mb_bf"], writes=["tb"])
                        P.op("dve", lambda eng: eng.tensor_tensor(
                            out=tf[:], in0=tb[:], in1=rb_bf[:], op=ALU.mult),
                            reads=["tb", "rb_bf"], writes=["tf"])
                        P.op("dve", lambda eng, c=c, blk=blk:
                             eng.tensor_tensor(
                                 out=xres_v[:, c, blk * 512:(blk + 1) * 512],
                                 in0=xres_v[:, c, blk * 512:(blk + 1) * 512],
                                 in1=tf[:], op=ALU.add),
                             reads=["tf", "xres"], writes=["xres"])
                P.fence()
        P.fence()
        if DEBUG:
            P.op("sync", lambda eng: eng.dma_start(out=dbg_xattn, in_=xres[:]),
                 reads=["xres"], writes=["dbgxa"], dma="qs")
        attn_scope.close()

        out_r = out_d.rearrange("p (c t) -> p c t", c=NCH)
        with ExitStack() as p3:
            S3 = lambda *a, **k: p3.enter_context(nc.sbuf_tensor(*a, **k))
            fuse = S3("fuse", [128, 64 * 512], BF16)
            dn = S3("dn", [128, NCH * 512], BF16)
            hln = S3("hln", [128, NCH * 512], BF16)
            wgb = [S3(f"wgb{i}", [128, 1024], BF16) for i in range(2)]
            wub = [S3(f"wub{i}", [128, 1024], BF16) for i in range(2)]
            wdb = [S3(f"wdb{i}", [128, 2048], BF16) for i in range(2)]
            sil = [S3(f"sil{i}", [128, 512], BF16) for i in range(2)]
            mb_bf3 = S3("mb_bf3", [128, 512], BF16)
            rb_bf3 = S3("rb_bf3", [128, 512], BF16)
            sqr = [S3(f"fsq{i}", [128, 512], BF16) for i in range(2)]
            tf2 = S3("tf2", [128, 512], F32)
            wd_q = wd_d.rearrange("a b p (c d) -> a b p c d", c=32)
            for blk in range(2):
                for c in range(NCH):
                    P.op("dve", lambda eng, c=c, blk=blk: eng.tensor_copy(
                        out=sil[c % 2][:],
                        in_=xres_v[:, c, blk * 512:(blk + 1) * 512]),
                        reads=["xres"], writes=[("sil", c % 2)])
                    P.op("pe", lambda eng, c=c: eng.matmul(
                        out=pb[6][0:1, 0:512], lhsT=ones_c_bf[:],
                        rhs=sil[c % 2][:], start=(c == 0), stop=(c == NCH - 1)),
                        reads=[("sil", c % 2), "ones"], writes=[PB(6)])
                    P.op("dve", lambda eng, c=c: eng.tensor_tensor(
                        out=sqr[c % 2][:], in0=sil[c % 2][:],
                        in1=sil[c % 2][:], op=ALU.mult),
                        reads=[("sil", c % 2)], writes=[("fsq", c % 2)])
                    P.op("pe", lambda eng, c=c: eng.matmul(
                        out=pb[7][0:1, 0:512], lhsT=ones_c_bf[:],
                        rhs=sqr[c % 2][:], start=(c == 0), stop=(c == NCH - 1)),
                        reads=[("fsq", c % 2), "ones"], writes=[PB(7)])
                ln_rows_finish(want_mb=True)
                if DEBUG:
                    pass
                for c in range(NCH):
                    P.op("pool", lambda eng, c=c, blk=blk:
                         eng.tensor_tensor(
                             out=hln[:, c * 512:(c + 1) * 512],
                             in0=xres_v[:, c, blk * 512:(blk + 1) * 512],
                             in1=mb_bc[:], op=ALU.subtract),
                         reads=["xres", "mb_bc"], writes=[("hln", c)])
                    P.op("dve", lambda eng, c=c: eng.tensor_tensor(
                        out=hln[:, c * 512:(c + 1) * 512],
                        in0=hln[:, c * 512:(c + 1) * 512],
                        in1=rb_bc[:], op=ALU.mult),
                        reads=[("hln", c), "rb_bc"], writes=[("hln", c)])
                hreads = [("hln", c) for c in range(NCH)]
                if DEBUG:
                    P.op("sync", lambda eng, blk=blk: eng.dma_start(
                        out=dbg_hln[blk], in_=hln[:]),
                        reads=hreads, writes=["dbghln"], dma="qs")
                for ic in range(64):
                    for h in range(2):
                        bi = (ic * 2 + h) % 2
                        P.op("pool", lambda eng, bi=bi, ic=ic, h=h:
                             eng.dma_start(
                                 out=wgb[bi][:], in_=wg_d[ic, h]),
                             writes=[("wgb", bi)], dma="qp")
                        P.op("pool", lambda eng, bi=bi, ic=ic, h=h:
                             eng.dma_start(
                                 out=wub[bi][:], in_=wu_d[ic, h]),
                             writes=[("wub", bi)], dma="qp")

                        def gmm(eng, bi=bi, h=h):
                            r = None
                            for c in range(8):
                                r = eng.matmul(
                                    out=pb[0][:, 0:512],
                                    lhsT=wgb[bi][:, c * 128:(c + 1) * 128],
                                    rhs=hln[:, (h * 8 + c) * 512:
                                             (h * 8 + c + 1) * 512],
                                    start=(h == 0 and c == 0),
                                    stop=(h == 1 and c == 7),
                                    skip_group_check=True)
                            return r
                        P.op("pe", gmm, reads=hreads + [("wgb", bi)],
                             writes=[PB(0)])

                        def umm(eng, bi=bi, h=h):
                            r = None
                            for c in range(8):
                                r = eng.matmul(
                                    out=pb[1][:, 0:512],
                                    lhsT=wub[bi][:, c * 128:(c + 1) * 128],
                                    rhs=hln[:, (h * 8 + c) * 512:
                                             (h * 8 + c + 1) * 512],
                                    start=(h == 0 and c == 0),
                                    stop=(h == 1 and c == 7),
                                    skip_group_check=True)
                            return r
                        P.op("pe", umm, reads=hreads + [("wub", bi)],
                             writes=[PB(1)])
                    P.op("act", lambda eng, ic=ic: eng.activation(
                        out=sil[ic % 2][:], in_=pb[0][:, 0:512],
                        func=AF.Silu, scale=sg_sb[:, ic:ic + 1]),
                        reads=[PB(0), ("sg", ic)],
                        writes=[("sil", ic % 2)])
                    P.op("dve", lambda eng, ic=ic:
                         eng.scalar_tensor_tensor(
                             out=fuse[:, ic * 512:(ic + 1) * 512],
                             in0=pb[1][:, 0:512],
                             scalar=su_sb[:, ic:ic + 1],
                             in1=sil[ic % 2][:], op0=ALU.mult,
                             op1=ALU.mult),
                         reads=[PB(1), ("sil", ic % 2), ("su", ic)],
                         writes=[("fuse", ic)])
                freads = [("fuse", ic) for ic in range(64)]
                for mp in range(8):
                    for q8 in range(8):
                        bi = q8 % 2
                        cih, half = divmod(q8, 4)
                        P.op("pool", lambda eng, bi=bi, mp=mp, cih=cih,
                             half=half: eng.dma_start(
                                 out=wdb[bi][:],
                                 in_=wd_q[mp, cih, :,
                                          half * 8:(half + 1) * 8, :]),
                             writes=[("wdb", bi)], dma="qp")
                        for dm in range(2):
                            m = mp * 2 + dm

                            def dmm(eng, bi=bi, q8=q8, dm=dm):
                                r = None
                                for cc in range(8):
                                    r = eng.matmul(
                                        out=pb[2 + dm][:, 0:512],
                                        lhsT=wdb[bi][:,
                                                     cc * 256 + dm * 128:
                                                     cc * 256 + (dm + 1) * 128],
                                        rhs=fuse[:,
                                                 (q8 * 8 + cc) * 512:
                                                 (q8 * 8 + cc + 1) * 512],
                                        start=(q8 == 0 and cc == 0),
                                        stop=(q8 == 7 and cc == 7),
                                        skip_group_check=True)
                                return r
                            P.op("pe", dmm,
                                 reads=freads + [("wdb", bi)],
                                 writes=[PB(2 + dm)])
                            if q8 == 7:
                                P.op("act", lambda eng, m=m, dm=dm:
                                     eng.activation(
                                         out=dn[:, m * 512:(m + 1) * 512],
                                         in_=pb[2 + dm][:, 0:512],
                                         func=AF.Copy,
                                         scale=sd_sb[:, m:m + 1]),
                                     reads=[PB(2 + dm), ("sd", m)],
                                     writes=[("dn", m)])
                if DEBUG:
                    P.op("sync", lambda eng, blk=blk: eng.dma_start(
                        out=dbg_dn[blk], in_=dn[:]),
                        reads=[("dn", c) for c in range(NCH)],
                        writes=["dbgdn"], dma="qs")
                ln_stats_rows([(dn[:, c * 512:(c + 1) * 512], ("dn", c))
                               for c in range(NCH)],
                              [(sqr[0][:], ("fsq", 0)),
                               (sqr[1][:], ("fsq", 1))])
                ln_rows_finish(mb_bf=(mb_bf3[:], "mb_bf3"),
                               rb_bf=(rb_bf3[:], "rb_bf3"))
                for c in range(NCH):
                    P.op("pool", lambda eng, c=c: eng.tensor_tensor(
                        out=sqr[0][:], in0=dn[:, c * 512:(c + 1) * 512],
                        in1=mb_bf3[:], op=ALU.subtract),
                        reads=[("dn", c), "mb_bf3"], writes=[("fsq", 0)])
                    P.op("dve", lambda eng: eng.tensor_tensor(
                        out=tf2[:], in0=sqr[0][:], in1=rb_bf3[:],
                        op=ALU.mult),
                        reads=[("fsq", 0), "rb_bf3"], writes=["tf2"])
                    P.op("dve", lambda eng, c=c, blk=blk:
                         eng.tensor_tensor(
                             out=xres_v[:, c, blk * 512:(blk + 1) * 512],
                             in0=xres_v[:, c, blk * 512:(blk + 1) * 512],
                             in1=tf2[:], op=ALU.add),
                         reads=["tf2", "xres"],
                         writes=["xres", ("xout", blk, c)])
                P.op("sync", lambda eng, blk=blk: eng.dma_start(
                    out=out_r[:, :, blk * 512:(blk + 1) * 512],
                    in_=xres_v[:, :, blk * 512:(blk + 1) * 512]),
                    reads=[("xout", blk, c) for c in range(NCH)],
                    writes=[("outd", blk)], dma="qs")
        xres_scope.close()

        P.tail_wait("sync", ["qs", "qp"])

        with nc.Block() as block:
            def run_stream(name):
                def _(eng):
                    for waits, emit, inc in P.ops[name]:
                        for (s, v) in waits:
                            eng.wait_ge(sem[s], v)
                        if emit is not None:
                            inst = emit(eng)
                            if inst is None:
                                raise RuntimeError(f"op on {name} returned None")
                            inst.then_inc(sem[inc[0]], inc[1])
                return _

            block.tensor(run_stream("pe"))
            block.vector(run_stream("dve"))
            block.scalar(run_stream("act"))
            block.gpsimd(run_stream("pool"))
            block.sync(run_stream("sync"))

    return nc


_NC_CACHE = None


def _get_nc():
    global _NC_CACHE
    if _NC_CACHE is None:
        _NC_CACHE = build_program()
    return _NC_CACHE


def kernel(L0_attn_normalized_resid_pre, L0_attn_W_Q, L0_attn_W_K, L0_attn_W_V,
           L0_attn_W_O, L0_ffn_w_gate, L0_ffn_w_up, L0_ffn_w_down,
           L0_ffn_s_gate, L0_ffn_s_up, L0_ffn_s_down):
    x = np.asarray(L0_attn_normalized_resid_pre)
    xT = np.ascontiguousarray(x.transpose(0, 2, 1))
    _t = lambda w, h: np.ascontiguousarray(np.asarray(w, dtype=np.float32).reshape(
        h, 16, 128, 64).transpose(0, 2, 1, 3).reshape(h, 128, 1024))
    wq = _t(L0_attn_W_Q, 32)
    wk = _t(L0_attn_W_K, 8)
    wv = _t(L0_attn_W_V, 8)
    wo = np.ascontiguousarray(
        np.asarray(L0_attn_W_O).reshape(512, 2048), dtype=np.float32)
    _ffn_tile = lambda w: np.ascontiguousarray(np.asarray(w).reshape(
        2, 8, 128, 64, 128).transpose(3, 0, 2, 1, 4).reshape(64, 2, 128, 1024))
    wg = _ffn_tile(L0_ffn_w_gate)
    wu = _ffn_tile(L0_ffn_w_up)
    wd = np.ascontiguousarray(np.asarray(L0_ffn_w_down).reshape(
        2, 32, 128, 8, 256).transpose(3, 0, 2, 1, 4).reshape(8, 2, 128, 8192))
    sg = np.ascontiguousarray(np.asarray(L0_ffn_s_gate), dtype=np.float32)
    su = np.ascontiguousarray(np.asarray(L0_ffn_s_up), dtype=np.float32)
    sd = np.ascontiguousarray(np.asarray(L0_ffn_s_down), dtype=np.float32)

    in_maps = []
    metas = []
    for core in range(8):
        s = core // 4
        j = core % 4
        A, B = j, 7 - j
        xq0 = np.concatenate(
            [xT[s][:, A * 512:(A + 1) * 512], xT[s][:, B * 512:(B + 1) * 512]],
            axis=1)
        xq = np.ascontiguousarray(
            xq0.reshape(16, 128, 2, 512).transpose(1, 0, 2, 3).reshape(128, 16384))
        others = [b for b in range(8) if b not in (A, B)]
        order = [b for b in others if b < A] + [b for b in others if b >= A]
        cols = [xT[s][:, b * 512:(b + 1) * 512] for b in order]
        cols.append(xT[s][:, B * 512:(B + 1) * 512])
        cols.append(xT[s][:, A * 512:(A + 1) * 512])
        xkv0 = np.concatenate(cols, axis=1)
        xkv = np.ascontiguousarray(
            xkv0.reshape(16, 128, 8, 512).transpose(2, 1, 0, 3).reshape(8, 128, 8192))
        mba = np.full((128, 12), ESHIFT - 10000.0, np.float32)
        mbb = np.full((128, 24), ESHIFT - 10000.0, np.float32)
        for t in range(12):
            if order[t // 4] < A:
                mba[:, t] = ESHIFT
        for t in range(24):
            if order[t // 4] < B:
                mbb[:, t] = ESHIFT
        in_maps.append({
            "xq": xq, "xkv": xkv, "wq": wq, "wk": wk, "wv": wv, "wo": wo,
            "wg": wg, "wu": wu, "wd": wd, "sg": sg, "su": su, "sd": sd,
            "mba": mba, "mbb": mbb,
        })
        metas.append((s, A, B))

    nc = _get_nc()
    res = run_bass_kernel_spmd(nc, in_maps, core_ids=list(range(8)))
    global LAST_RESULTS
    LAST_RESULTS = res if DEBUG else None

    outT = np.empty((2, 2048, 4096), np.float32)
    for core, (s, A, B) in enumerate(metas):
        o = np.asarray(res.results[core]["out"]).reshape(128, 16, 2, 512)
        outT[s][:, A * 512:(A + 1) * 512] = \
            o[:, :, 0, :].transpose(1, 0, 2).reshape(2048, 512)
        outT[s][:, B * 512:(B + 1) * 512] = \
            o[:, :, 1, :].transpose(1, 0, 2).reshape(2048, 512)
    return np.ascontiguousarray(outT.transpose(0, 2, 1))
